# version 14
# speedup vs baseline: 2.4113x; 1.1228x over previous
import numpy as np
import ml_dtypes
import concourse.bass as bass
import concourse.mybir as mybir
from concourse.bass_utils import run_bass_kernel_spmd
from concourse.bass_core_rust import IndirectOffsetOnAxis
from contextlib import ExitStack

F32 = mybir.dt.float32
BF16 = mybir.dt.bfloat16
I32 = mybir.dt.int32
AF = mybir.ActivationFunctionType
ALU = mybir.AluOpType
AX = mybir.AxisListType

SEQ = 2048
D = 1024
NT = SEQ // 128
NSEQ = 2
H = 16
I3 = 3072
FI = 2816
NKC = D // 128
NIC = FI // 128
SCALE = 0.125

TL = [0.0625, 0.5625, 1.5625, 3.0625]
TSQ = [6.25, 12.25, 25.0]
QUANT_INCS = 19
QPRE_INCS = 5
QPOST_U = 7
QSC_INCS = 4
QPL_INCS = 3


class Sem:
    def __init__(self, handle):
        self.h = handle
        self.v = 0

    def inc(self, inst, n=1):
        inst.then_inc(self.h, n)
        self.v += n
        return self.v


def _emit_quant(vec, x_ap, nblk, bstats, v, a2, L, S, out_bf, sv):
    n = 16 * nblk
    sc = bstats[:, 0:nblk]
    rsc = bstats[:, 176:176 + nblk]
    hsc = bstats[:, 352:352 + nblk]
    seg = lambda ap: ap.rearrange("p (g s) -> p g s", s=16)
    bc = lambda ap: ap.rearrange("p g -> p g ()").broadcast_to((128, nblk, 16))
    vec.wait_ge(sv.h, sv.v)
    sv.inc(vec.tensor_reduce(sc, seg(x_ap), AX.X, ALU.max, apply_absolute_value=True))
    vec.wait_ge(sv.h, sv.v)
    sv.inc(vec.tensor_scalar(sc, sc, 1.0 / 6.0, 1e-12, ALU.mult, ALU.max))
    vec.wait_ge(sv.h, sv.v)
    sv.inc(vec.reciprocal(rsc, sc))
    sv.inc(vec.tensor_scalar_mul(hsc, sc, 0.5))
    vec.wait_ge(sv.h, sv.v)
    sv.inc(vec.tensor_tensor(seg(v[:, 0:n]), seg(x_ap), bc(rsc), ALU.mult))
    vec.wait_ge(sv.h, sv.v)
    sv.inc(vec.tensor_mul(a2[:, 0:n], v[:, 0:n], v[:, 0:n]))
    vec.wait_ge(sv.h, sv.v)
    sv.inc(vec.tensor_single_scalar(L[:, 0:n], a2[:, 0:n], TL[0], ALU.is_gt))
    for c in TL[1:]:
        vec.wait_ge(sv.h, sv.v)
        sv.inc(vec.scalar_tensor_tensor(L[:, 0:n], a2[:, 0:n], c, L[:, 0:n], ALU.is_gt, ALU.add))
    sv.inc(vec.tensor_single_scalar(S[:, 0:n], a2[:, 0:n], TSQ[2], ALU.is_gt))
    for c in (TSQ[1], TSQ[0], TSQ[2]):
        vec.wait_ge(sv.h, sv.v)
        sv.inc(vec.scalar_tensor_tensor(S[:, 0:n], a2[:, 0:n], c, S[:, 0:n], ALU.is_gt, ALU.add))
    vec.wait_ge(sv.h, sv.v)
    sv.inc(vec.scalar_tensor_tensor(L[:, 0:n], S[:, 0:n], 2.0, L[:, 0:n], ALU.mult, ALU.add))
    sv.inc(vec.tensor_single_scalar(a2[:, 0:n], v[:, 0:n], 0.0, ALU.is_gt))
    vec.wait_ge(sv.h, sv.v)
    sv.inc(vec.tensor_scalar(a2[:, 0:n], a2[:, 0:n], 2.0, -1.0, ALU.mult, ALU.add))
    vec.wait_ge(sv.h, sv.v)
    sv.inc(vec.tensor_mul(L[:, 0:n], L[:, 0:n], a2[:, 0:n]))
    vec.wait_ge(sv.h, sv.v)
    sv.inc(vec.tensor_tensor(seg(out_bf), seg(L[:, 0:n]), bc(hsc), ALU.mult))


def _q_pre(vec, x_ap, nblk, bstats, boff, v, sv):
    n = 16 * nblk
    sc = bstats[:, boff:boff + nblk]
    rsc = bstats[:, 176 + boff:176 + boff + nblk]
    hsc = bstats[:, 352 + boff:352 + boff + nblk]
    seg = lambda ap: ap.rearrange("p (g s) -> p g s", s=16)
    bc = lambda ap: ap.rearrange("p g -> p g ()").broadcast_to((128, nblk, 16))
    vec.wait_ge(sv.h, sv.v)
    sv.inc(vec.tensor_reduce(sc, seg(x_ap), AX.X, ALU.max, apply_absolute_value=True))
    vec.wait_ge(sv.h, sv.v)
    sv.inc(vec.tensor_scalar(sc, sc, 1.0 / 6.0, 1e-12, ALU.mult, ALU.max))
    vec.wait_ge(sv.h, sv.v)
    sv.inc(vec.reciprocal(rsc, sc))
    sv.inc(vec.tensor_scalar_mul(hsc, sc, 0.5))
    vec.wait_ge(sv.h, sv.v)
    sv.inc(vec.tensor_tensor(seg(v[:, 0:n]), seg(x_ap), bc(rsc), ALU.mult))


def _q_scal(scal, v_ap, a2_ap, s1_ap, s2_ap, s3_ap, qbias, ss,
            s_wait=None, s3_wait=None):
    ss.inc(scal.activation(a2_ap, v_ap, AF.Square))
    scal.wait_ge(ss.h, ss.v)
    if s_wait is not None:
        scal.wait_ge(s_wait[0], s_wait[1])
    ss.inc(scal.activation(s1_ap, a2_ap, AF.Sign, bias=qbias[:, 0:1]))
    ss.inc(scal.activation(s2_ap, a2_ap, AF.Sign, bias=qbias[:, 1:2]))
    if s3_wait is not None:
        scal.wait_ge(s3_wait[0], s3_wait[1])
    ss.inc(scal.activation(s3_ap, a2_ap, AF.Sign, bias=qbias[:, 2:3]))


def _q_pool(g, s1, s2, s3, u, n, sq):
    sq.inc(g.tensor_tensor(u[:, 0:n], s1[:, 0:n], s2[:, 0:n], ALU.add))
    g.wait_ge(sq.h, sq.v)
    sq.inc(g.tensor_tensor(u[:, 0:n], u[:, 0:n], s3[:, 0:n], ALU.add))
    g.wait_ge(sq.h, sq.v)
    sq.inc(g.tensor_tensor(u[:, 0:n], u[:, 0:n], s3[:, 0:n], ALU.add))
    g.wait_ge(sq.h, sq.v)


def _q_post(vec, nblk, bstats, boff, a2, sgn, u, L, out_bf, sv, sgn_wait=None,
            s123=None):
    n = 16 * nblk
    hsc = bstats[:, 352 + boff:352 + boff + nblk]
    seg = lambda ap: ap.rearrange("p (g s) -> p g s", s=16)
    bc = lambda ap: ap.rearrange("p g -> p g ()").broadcast_to((128, nblk, 16))
    vec.wait_ge(sv.h, sv.v)
    sv.inc(vec.tensor_scalar(L[:, 0:n], a2[:, 0:n], TL[0], 4.0, ALU.is_gt, ALU.add))
    for c in TL[1:]:
        vec.wait_ge(sv.h, sv.v)
        sv.inc(vec.scalar_tensor_tensor(L[:, 0:n], a2[:, 0:n], c, L[:, 0:n], ALU.is_gt, ALU.add))
    if u is not None:
        vec.wait_ge(sv.h, sv.v)
        sv.inc(vec.scalar_tensor_tensor(L[:, 0:n], u[:, 0:n], 0.0, L[:, 0:n], ALU.add, ALU.add))
    else:
        s1, s2, s3 = s123
        vec.wait_ge(sv.h, sv.v)
        sv.inc(vec.scalar_tensor_tensor(L[:, 0:n], s1[:, 0:n], 0.0, L[:, 0:n], ALU.add, ALU.add))
        vec.wait_ge(sv.h, sv.v)
        sv.inc(vec.scalar_tensor_tensor(L[:, 0:n], s2[:, 0:n], 0.0, L[:, 0:n], ALU.add, ALU.add))
        vec.wait_ge(sv.h, sv.v)
        sv.inc(vec.scalar_tensor_tensor(L[:, 0:n], s3[:, 0:n], 2.0, L[:, 0:n], ALU.mult, ALU.add))
    vec.wait_ge(sv.h, sv.v)
    if sgn_wait is not None:
        vec.wait_ge(sgn_wait[0], sgn_wait[1])
    sv.inc(vec.tensor_mul(L[:, 0:n], L[:, 0:n], sgn[:, 0:n]))
    vec.wait_ge(sv.h, sv.v)
    sv.inc(vec.tensor_tensor(seg(out_bf), seg(L[:, 0:n]), bc(hsc), ALU.mult))


def build_nc(stage=9):
    bsub = stage - 30 if 30 <= stage < 40 else 99
    nc = bass.Bass("TRN2", target_bir_lowering=False)

    xin = nc.dram_tensor("xin", [NSEQ * SEQ, D], F32, kind="ExternalInput").ap()
    Wqkv = nc.dram_tensor("Wqkv", [D, I3], F32, kind="ExternalInput").ap()
    bqkv = nc.dram_tensor("bqkv", [1, I3], F32, kind="ExternalInput").ap()
    Wproj = nc.dram_tensor("Wproj", [D, D], F32, kind="ExternalInput").ap()
    bproj = nc.dram_tensor("bproj", [1, D], F32, kind="ExternalInput").ap()
    mapping = nc.dram_tensor("mapping", [4096, 1], I32, kind="ExternalInput").ap()
    wg = nc.dram_tensor("wg", [FI, D], F32, kind="ExternalInput").ap()
    wu = nc.dram_tensor("wu", [FI, D], F32, kind="ExternalInput").ap()
    wd = nc.dram_tensor("wd", [D, FI], F32, kind="ExternalInput").ap()
    ident_in = nc.dram_tensor("ident_in", [128, 128], BF16, kind="ExternalInput").ap()
    ones_in = nc.dram_tensor("ones_in", [1, 128], BF16, kind="ExternalInput").ap()
    etop_in = nc.dram_tensor("etop_in", [1, 128], BF16, kind="ExternalInput").ap()
    ebot_in = nc.dram_tensor("ebot_in", [1, 128], BF16, kind="ExternalInput").ap()
    ssel_in = nc.dram_tensor("ssel_in", [128, 128], BF16, kind="ExternalInput").ap()

    out = nc.dram_tensor("out", [NSEQ * SEQ, D], F32, kind="ExternalOutput").ap()

    qkv_dram = nc.dram_tensor("qkv_dram", [SEQ, I3], BF16, kind="Internal").ap()
    partial_dram = nc.dram_tensor("partial_dram", [NSEQ * SEQ, D], F32, kind="Internal").ap()
    xqT_dram = nc.dram_tensor("xqT_dram", [NSEQ, NKC, 128, SEQ], BF16, kind="Internal").ap()
    kidx_dram = nc.dram_tensor("kidx_dram", [SEQ, 1], I32, kind="Internal").ap()

    with ExitStack() as st:
        sb = lambda n, s, d: st.enter_context(nc.sbuf_tensor(n, s, d))
        psb = lambda n, s, d: st.enter_context(nc.psum_tensor(n, s, d))
        mksem = lambda n: Sem(st.enter_context(nc.semaphore(n)))

        ld = mksem("ld")
        lg = mksem("lg")
        sv = mksem("sv")
        ss = mksem("ss")
        stn = mksem("stn")
        sout = mksem("sout")
        sp = mksem("sp")
        sq = mksem("sq")

        ident = sb("ident", [128, 128], BF16)
        ones_bf = sb("ones_bf", [1, 128], BF16)
        etop = sb("etop", [1, 128], BF16)
        ebot = sb("ebot", [1, 128], BF16)
        ssel = sb("ssel", [128, 128], BF16)
        onescol = sb("onescol", [128, 1], BF16)

        qidx = sb("qidx", [128, 16], I32)
        kidx = sb("kidx", [128, 16], I32)
        uneg = sb("uneg", [128, 16], F32)
        maskU = sb("maskU", [128, 1024], BF16)

        sc0 = sb("sc0", [128, 1024], F32)
        sc1 = sb("sc1", [128, 1024], F32)
        sc2 = sb("sc2", [128, 1024], F32)
        sc3 = sb("sc3", [128, 1024], F32)
        sc4 = sb("sc4", [128, 1024], F32)
        sc5 = sb("sc5", [128, 1024], F32)
        sc6 = sb("sc6", [128, 1024], F32)
        bstats = sb("bstats", [128, 528], F32)
        qbias = sb("qbias", [128, 3], F32)
        stats = sb("stats", [128, 16], F32)

        sexp = sb("sexp", [128, 1024], BF16)
        attn_tm = sb("attn_tm", [128, 1024], BF16)
        r_sb = sb("r_sb", [128, 16], F32)

        psQ = psb("psQ", [128, 512], F32)
        psU = psb("psU", [128, 512], F32)
        psA = psb("psA", [128, 1024], F32)
        psB = psb("psB", [128, 1024], F32)
        psT = psb("psT", [128, 1024], BF16)
        psS = psb("psS", [128, 32], F32)

        g0 = ExitStack()
        g0sb = lambda n, s, d: g0.enter_context(nc.sbuf_tensor(n, s, d))
        map_col = g0sb("map_col", [128, 16], I32)
        map_colf = g0sb("map_colf", [128, 16], F32)
        u_col = g0sb("u_col", [128, 16], F32)
        u_colbf = g0sb("u_colbf", [128, 16], BF16)
        m64 = g0sb("m64", [32, 1], I32)
        m64f = g0sb("m64f", [32, 4], F32)
        m64i = g0sb("m64i", [32, 1], I32)
        mseg = g0sb("mseg", [32, 256], I32)
        msegf = g0sb("msegf", [32, 256], F32)
        kidx32 = g0sb("kidx32", [32, 64], F32)
        kidx32i = g0sb("kidx32i", [32, 64], I32)
        map_row = g0sb("map_row", [1, 2048], I32)
        map_rowf = g0sb("map_rowf", [1, 2048], F32)
        mask_row = g0sb("mask_row", [1, 2048], BF16)
        mrow_a = g0sb("mrow_a", [1, 512], BF16)
        mrow_b = g0sb("mrow_b", [1, 512], BF16)
        with nc.Block() as blk:
            @blk.sync
            def _(sync):
                sync.dma_start(out=ident[:], in_=ident_in[:, :]).then_inc(ld.h, 16)
                sync.dma_start(out=ones_bf[:], in_=ones_in[:, :]).then_inc(ld.h, 16)
                sync.dma_start(out=etop[:], in_=etop_in[:, :]).then_inc(ld.h, 16)
                sync.dma_start(out=ebot[:], in_=ebot_in[:, :]).then_inc(ld.h, 16)
                sync.dma_start(out=ssel[:], in_=ssel_in[:, :]).then_inc(ld.h, 16)
                sync.dma_start(out=map_row[:],
                               in_=mapping[0:2048, 0].rearrange("(o t) -> o t", o=1)).then_inc(ld.h, 16)
                sync.dma_start(out=map_col[:],
                               in_=mapping[0:2048, 0].rearrange("(j p) -> p j", p=128),
                               allow_slow_non_contiguous=True).then_inc(ld.h, 16)
                sync.dma_start(out=m64[:],
                               in_=mapping[:, 0].rearrange("(n s) -> n s", s=64)[0:32, 0:1],
                               allow_slow_non_contiguous=True).then_inc(ld.h, 16)
                ld.v += 16 * 8

            @blk.vector
            def _(vec):
                vec.wait_ge(ld.h, ld.v)
                sv.inc(vec.memset(onescol[:], 1.0))
                sv.inc(vec.memset(qbias[:, 0:1], -TSQ[0]))
                sv.inc(vec.memset(qbias[:, 1:2], -TSQ[1]))
                sv.inc(vec.memset(qbias[:, 2:3], -TSQ[2]))
                sv.inc(vec.tensor_copy(map_colf[:], map_col[:]))
                vec.wait_ge(sv.h, sv.v)
                sv.inc(vec.tensor_scalar_min(map_colf[:], map_colf[:], 2047.0))
                vec.wait_ge(sv.h, sv.v)
                sv.inc(vec.tensor_copy(qidx[:], map_colf[:]))
                sv.inc(vec.tensor_copy(u_col[:], map_col[:]))
                vec.wait_ge(sv.h, sv.v)
                sv.inc(vec.tensor_single_scalar(u_col[:], u_col[:], 2048.0, ALU.is_lt))
                vec.wait_ge(sv.h, sv.v)
                sv.inc(vec.tensor_copy(u_colbf[:], u_col[:]))
                sv.inc(vec.tensor_copy(m64f[:, 0:1], m64[:]))
                vec.wait_ge(sv.h, sv.v)
                sv.inc(vec.tensor_single_scalar(m64f[:, 0:1], m64f[:, 0:1], 1.0 / 256.0, ALU.mult))
                vec.wait_ge(sv.h, sv.v)
                sv.inc(vec.tensor_copy(m64i[:], m64f[:, 0:1]))
                vec.wait_ge(sv.h, sv.v)
                sv.inc(vec.tensor_copy(m64f[:, 1:2], m64i[:]))
                vec.wait_ge(sv.h, sv.v)
                sv.inc(vec.tensor_tensor(m64f[:, 2:3], m64f[:, 1:2], m64f[:, 0:1], ALU.is_gt))
                vec.wait_ge(sv.h, sv.v)
                sv.inc(vec.tensor_sub(m64f[:, 3:4], m64f[:, 1:2], m64f[:, 2:3]))
                vec.wait_ge(sv.h, sv.v)
                sv.inc(vec.tensor_single_scalar(m64f[:, 3:4], m64f[:, 3:4], 256.0, ALU.mult))
                vec.wait_ge(sv.h, sv.v)
                sv.inc(vec.tensor_copy(m64i[:], m64f[:, 3:4]))
                sv.inc(vec.tensor_copy(map_rowf[:], map_row[:]))
                vec.wait_ge(sv.h, sv.v)
                sv.inc(vec.tensor_single_scalar(map_rowf[:], map_rowf[:], 2048.0, ALU.is_lt))
                vec.wait_ge(sv.h, sv.v)
                sv.inc(vec.tensor_scalar(map_rowf[:], map_rowf[:], -1.0, 8e9, ALU.add, ALU.mult))
                vec.wait_ge(sv.h, sv.v)
                sv.inc(vec.tensor_copy(mask_row[:], map_rowf[:]))

        with nc.Block() as blk:
            @blk.gpsimd
            def _(gpsimd):
                gpsimd.indirect_dma_start(
                    mseg[:], None, mapping[:, :],
                    IndirectOffsetOnAxis(m64i[:], 0)).then_inc(lg.h, 16)
                lg.v += 16
                gpsimd.wait_ge(lg.h, lg.v)
                gpsimd.sem_inc(sp.h, 1)
                sp.v += 1

            @blk.vector
            def _(vec):
                vec.wait_ge(sp.h, sp.v)
                sv.inc(vec.tensor_copy(msegf[:], mseg[:]))
                vec.wait_ge(sv.h, sv.v)
                sv.inc(vec.tensor_scalar_min(kidx32[:], msegf[:, 0:256:4], 2047.0))
                vec.wait_ge(sv.h, sv.v)
                sv.inc(vec.tensor_copy(kidx32i[:], kidx32[:]))

        with nc.Block() as blk:
            @blk.gpsimd
            def _(gpsimd):
                gpsimd.dma_start(out=kidx_dram[:, 0].rearrange("(p f) -> p f", p=32),
                                 in_=kidx32i[:]).then_inc(lg.h, 16)
                lg.v += 16
                gpsimd.wait_ge(lg.h, lg.v)
                gpsimd.dma_start(out=kidx[:],
                                 in_=kidx_dram[:, 0].rearrange("(j p) -> p j", p=128),
                                 allow_slow_non_contiguous=True).then_inc(lg.h, 16)
                lg.v += 16
                gpsimd.wait_ge(lg.h, lg.v)

            @blk.vector
            def _(vec):
                src = mask_row[0:1, :].rearrange("o (j q) -> o j q", q=128)
                sv.inc(vec.tensor_copy(mrow_a[0:1, :].rearrange("o (j q) -> o j q", q=64),
                                       src[:, 0:8, 0:64]))
                sv.inc(vec.tensor_copy(mrow_b[0:1, :].rearrange("o (j q) -> o j q", q=64),
                                       src[:, 0:8, 64:128]))

        with nc.Block() as blk:
            @blk.tensor
            def _(tensor):
                tensor.matmul(psA[:, 0:512], etop[0:1, :], mrow_a[0:1, :], start=True, stop=False)
                stn.inc(tensor.matmul(psA[:, 0:512], ebot[0:1, :], mrow_b[0:1, :], start=False, stop=True))
                stn.inc(tensor.matmul(psS[:, 0:16], ssel[:], u_colbf[:], start=True, stop=True))

            @blk.vector
            def _(vec):
                vec.wait_ge(stn.h, stn.v)
                sv.inc(vec.tensor_copy(maskU[:, 0:512], psA[:, 0:512]))
                sv.inc(vec.tensor_scalar(uneg[:], psS[:, 0:16], -1.0, 1e10, ALU.add, ALU.mult))
                src = mask_row[0:1, :].rearrange("o (j q) -> o j q", q=128)
                sv.inc(vec.tensor_copy(mrow_a[0:1, :].rearrange("o (j q) -> o j q", q=64),
                                       src[:, 8:16, 0:64]))
                sv.inc(vec.tensor_copy(mrow_b[0:1, :].rearrange("o (j q) -> o j q", q=64),
                                       src[:, 8:16, 64:128]))

        with nc.Block() as blk:
            @blk.tensor
            def _(tensor):
                tensor.matmul(psA[:, 512:1024], etop[0:1, :], mrow_a[0:1, :], start=True, stop=False)
                stn.inc(tensor.matmul(psA[:, 512:1024], ebot[0:1, :], mrow_b[0:1, :], start=False, stop=True))

            @blk.vector
            def _(vec):
                vec.wait_ge(stn.h, stn.v)
                sv.inc(vec.tensor_copy(maskU[:, 512:1024], psA[:, 512:1024]))

        g0.close()

        with ExitStack() as g1:
            g1sb = lambda n, s, d: g1.enter_context(nc.sbuf_tensor(n, s, d))
            xnT = g1sb("xnT", [128, 1024], BF16)
            xq_tm = g1sb("xq_tm", [128, 1024], BF16)
            Wqkv_sb = g1sb("Wqkv_sb", [128, NKC, I3], BF16)
            Wproj_sb = g1sb("Wproj_sb", [128, NKC, D], BF16)
            attn_fm = g1sb("attn_fm", [128, NKC, SEQ], BF16)
            Qsel = g1sb("Qsel", [128, 2, 1024], BF16)
            Ksel = g1sb("Ksel", [128, 2, 1024], BF16)
            Vsel = g1sb("Vsel", [128, 2, 1024], BF16)
            sexp2 = g1sb("sexp2", [128, 1024], BF16)
            attn_tm2 = g1sb("attn_tm2", [128, 1024], BF16)
            QselT = g1sb("QselT", [128, 1024], BF16)
            KselT = g1sb("KselT", [128, 1024], BF16)
            qkv_tile = g1sb("qkv_tile", [128, I3], BF16)
            qkv_tile2 = g1sb("qkv_tile2", [128, I3], BF16)
            xnT2 = g1sb("xnT2", [128, 1024], BF16)
            xq_tm2 = g1sb("xq_tm2", [128, 1024], BF16)
            xqTst = g1sb("xqTst", [128, 1024], BF16)
            xqTst2 = g1sb("xqTst2", [128, 1024], BF16)
            bqkv_bf = g1sb("bqkv_bf", [1, I3], BF16)
            bproj_bf = g1sb("bproj_bf", [1, D], BF16)

            with nc.Block() as blk:
                @blk.gpsimd
                def _(gpsimd):
                    gpsimd.dma_start(out=bqkv_bf[:], in_=bqkv[:, :]).then_inc(lg.h, 16)
                    gpsimd.dma_start(out=bproj_bf[:], in_=bproj[:, :]).then_inc(lg.h, 16)
                    for c in range(NKC):
                        gpsimd.dma_start(out=Wqkv_sb[:, c, :],
                                         in_=Wqkv[128 * c:128 * (c + 1), :]).then_inc(lg.h, 16)
                        gpsimd.dma_start(out=Wproj_sb[:, c, :],
                                         in_=Wproj[128 * c:128 * (c + 1), :]).then_inc(lg.h, 16)
                    lg.v += 16 * (2 + 2 * NKC)
                    gpsimd.wait_ge(lg.h, lg.v)

            for seq in range(NSEQ if stage >= 5 else (1 if (stage >= 2 or stage >= 30) else 0)):
                if True:
                    T = NT
                    xbA = [sc0, sc1]
                    xnbfA = [sexp, attn_tm]
                    xqA = [xq_tm, xq_tm2]
                    xnTA = [xnT, xnT2]
                    xqTA = [xqTst, xqTst2]
                    qkvA = [qkv_tile, qkv_tile2]
                    svA, ssA, stnA, ldA, lgA, spA = sv.v, ss.v, stn.v, ld.v, lg.v, sp.v

                    def cva(t):
                        return svA + 26 * t

                    def csa(t):
                        return ssA + (9 * t - 8 if t >= 1 else 0)

                    def cpa(t):
                        return stnA + (8 * t - 6 if t >= 1 else 0)

                    def sc_off(t):
                        return 1 if t <= T - 1 else 0

                    rstdv = lambda t: csa(t) + 1
                    xnTev = lambda tm1: csa(tm1 + 1) + sc_off(tm1 + 1) + 1
                    qkvev = lambda tm1, oc: csa(tm1 + 1) + sc_off(tm1 + 1) + 2 + oc
                    xqTev = lambda tm1: csa(tm1 + 1) + sc_off(tm1 + 1) + 8
                    qkvchunk = lambda tm1, oc: cpa(tm1 + 1) + oc + 1
                    txnv = lambda t: cpa(t) + (6 if t >= 1 else 0) + 1
                    txqv = lambda t: txnv(t) + 1
                    ldav = lambda t: ldA + 16 * (t + 1)

                    with nc.Block() as blk:
                        @blk.sync
                        def _(sync):
                            for t in range(T):
                                if t >= 1:
                                    sync.wait_ge(ld.h, ldav(t - 1))
                                if t >= 2:
                                    sync.wait_ge(sv.h, cva(t - 2) + 6)
                                sync.dma_start(out=xbA[t % 2][:],
                                               in_=xin[seq * SEQ + 128 * t:seq * SEQ + 128 * t + 128, :]).then_inc(ld.h, 16)
                                ld.v += 16

                        @blk.vector
                        def _(vec):
                            for t in range(T):
                                assert sv.v == cva(t), (t, sv.v)
                                vec.wait_ge(ld.h, ldav(t))
                                sv.inc(vec.bn_stats(stats[:, 0:6], xbA[t % 2][:, 0:512]))
                                sv.inc(vec.bn_stats(stats[:, 6:12], xbA[t % 2][:, 512:1024]))
                                vec.wait_ge(sv.h, sv.v)
                                sv.inc(vec.bn_aggr(stats[:, 12:14], stats[:, 0:12]))
                                vec.wait_ge(sv.h, sv.v)
                                sv.inc(vec.tensor_scalar_add(stats[:, 14:15], stats[:, 13:14], 1e-5))
                                vec.wait_ge(sv.h, sv.v)
                                sv.inc(vec.reciprocal(stats[:, 15:16], stats[:, 14:15]))
                                vec.wait_ge(ss.h, rstdv(t))
                                sv.inc(vec.tensor_scalar(sc2[:], xbA[t % 2][:], stats[:, 12:13], stats[:, 14:15],
                                                         ALU.subtract, ALU.mult))
                                if t >= 2:
                                    vec.wait_ge(stn.h, txnv(t - 2))
                                vec.wait_ge(sv.h, sv.v)
                                sv.inc(vec.tensor_copy(xnbfA[t % 2][:], sc2[:]))
                                if t >= 2:
                                    vec.wait_ge(stn.h, txqv(t - 2))
                                _emit_quant(vec, sc2[:], 64, bstats, sc3, sc4, sc5, sc6,
                                            xqA[t % 2][:], sv)

                        @blk.scalar
                        def _(scal):
                            for t in range(T + 1):
                                assert ss.v == csa(t), (t, ss.v)
                                if t <= T - 1:
                                    scal.wait_ge(sv.h, cva(t) + 5)
                                    ss.inc(scal.activation(stats[:, 14:15], stats[:, 15:16], AF.Sqrt))
                                if t >= 1:
                                    scal.wait_ge(stn.h, txnv(t - 1))
                                    ss.inc(scal.copy(xnTA[(t - 1) % 2][:], psT[:]))
                                    for oc in range(6):
                                        scal.wait_ge(stn.h, qkvchunk(t - 1, oc))
                                        if t >= 2 and oc == 0:
                                            scal.wait_ge(sp.h, spA + t - 1)
                                        pq = psQ if oc % 2 == 0 else psU
                                        ss.inc(scal.copy(qkvA[(t - 1) % 2][:, 512 * oc:512 * (oc + 1)],
                                                         pq[:]))
                                    scal.wait_ge(stn.h, txqv(t - 1))
                                    ss.inc(scal.copy(xqTA[(t - 1) % 2][:], psT[:]))

                        @blk.tensor
                        def _(tensor):
                            for t in range(T + 1):
                                assert stn.v == cpa(t), (t, stn.v)
                                if t >= 1:
                                    tensor.wait_ge(ss.h, xnTev(t - 1))
                                    for oc in range(6):
                                        if oc >= 2:
                                            tensor.wait_ge(ss.h, qkvev(t - 1, oc - 2))
                                        pq = psQ if oc % 2 == 0 else psU
                                        for c in range(NKC):
                                            tensor.matmul(pq[:], xnTA[(t - 1) % 2][:, 128 * c:128 * (c + 1)],
                                                          Wqkv_sb[:, c, 512 * oc:512 * (oc + 1)],
                                                          start=(c == 0), stop=False)
                                        stn.inc(tensor.matmul(pq[:], ones_bf[0:1, :],
                                                              bqkv_bf[0:1, 512 * oc:512 * (oc + 1)],
                                                              start=False, stop=True))
                                if t <= T - 1:
                                    vwait = cva(t) + 7
                                    tensor.wait_ge(sv.h, vwait)
                                    if t >= 1:
                                        tensor.wait_ge(ss.h, xqTev(t - 1))
                                    m = None
                                    for c in range(NKC):
                                        m = tensor.transpose(psT[:, 128 * c:128 * (c + 1)],
                                                             xnbfA[t % 2][:, 128 * c:128 * (c + 1)], ident[:])
                                    stn.inc(m)
                                    tensor.wait_ge(sv.h, cva(t + 1))
                                    tensor.wait_ge(ss.h, xnTev(t))
                                    m = None
                                    for c in range(NKC):
                                        m = tensor.transpose(psT[:, 128 * c:128 * (c + 1)],
                                                             xqA[t % 2][:, 128 * c:128 * (c + 1)], ident[:])
                                    stn.inc(m)

                        @blk.gpsimd
                        def _(gpsimd):
                            for t in range(1, T + 1):
                                gpsimd.wait_ge(ss.h, xqTev(t - 1))
                                gpsimd.dma_start(out=qkv_dram[128 * (t - 1):128 * t, :],
                                                 in_=qkvA[(t - 1) % 2][:]).then_inc(lg.h, 16)
                                gpsimd.dma_start(
                                    out=xqT_dram[seq, :, :, 128 * (t - 1):128 * t].rearrange("c p t -> p c t"),
                                    in_=xqTA[(t - 1) % 2][:].rearrange("p (c t) -> p c t", t=128)).then_inc(lg.h, 16)
                                lg.v += 32
                                gpsimd.wait_ge(lg.h, lg.v)
                                gpsimd.sem_inc(sp.h, 1)
                                sp.v += 1

                if stage >= 3 or stage >= 30:
                    T = NT
                    sxB = [sexp, sexp2]
                    atB = [attn_tm, attn_tm2]
                    svB, ssB, stnB, lgB, spB = sv.v, ss.v, stn.v, lg.v, sp.v

                    EV = {}

                    def prepass():
                        cs, cv, cp, csp = ssB, svB, stnB, spB
                        for j in range(T + 2):
                            if j <= T - 1:
                                csp += 1
                                EV[("gath", j)] = csp
                            if j >= 2:
                                cp += 1; EV[("attnT", j - 2)] = cp
                            if j <= T - 1:
                                cp += 1; EV[("tq", j)] = cp
                                cp += 1; EV[("tk", j)] = cp
                                cp += 2; EV[("scores", j)] = cp
                            if 1 <= j <= T:
                                cp += 2; EV[("pv", j - 1)] = cp
                            if j >= 2:
                                cs += 1; EV[("fmcopy", j - 2)] = cs
                            if j <= T - 1:
                                cs += 1; EV[("evq", j)] = cs
                                cs += 1; EV[("evk", j)] = cs
                                cs += 1; EV[("exp", j)] = cs
                            if j <= T - 1:
                                cv += 1; EV[("masked", j)] = cv
                                cv += 1; EV[("colmax", j)] = cv
                                cv += 1; EV[("subm", j)] = cv
                            if 1 <= j <= T:
                                cv += 1; EV[("radd", j - 1)] = cv
                                cv += 1; EV[("recip", j - 1)] = cv
                                cv += 2; EV[("evacTT", j - 1)] = cv
                        EV["end_cp"] = cp
                        EV["end_cs"] = cs
                        EV["end_cv"] = cv
                        EV["end_csp"] = csp
                    prepass()

                    with nc.Block() as blk:
                        @blk.gpsimd
                        def _(gpsimd):
                            for j in range(T):
                                if j >= 2:
                                    gpsimd.wait_ge(stn.h, EV[("pv", j - 2)])
                                b = j % 2
                                gpsimd.indirect_dma_start(
                                    Qsel[:, b, :], None, qkv_dram[:, :],
                                    IndirectOffsetOnAxis(qidx[:, j:j + 1], 0)).then_inc(lg.h, 16)
                                gpsimd.indirect_dma_start(
                                    Ksel[:, b, :], None, qkv_dram[:, :],
                                    IndirectOffsetOnAxis(kidx[:, j:j + 1], 0),
                                    element_offset=1024).then_inc(lg.h, 16)
                                gpsimd.indirect_dma_start(
                                    Vsel[:, b, :], None, qkv_dram[:, :],
                                    IndirectOffsetOnAxis(kidx[:, j:j + 1], 0),
                                    element_offset=2048).then_inc(lg.h, 16)
                                lg.v += 48
                                gpsimd.wait_ge(lg.h, lg.v)
                                gpsimd.sem_inc(sp.h, 1)
                                sp.v += 1
                                assert sp.v == EV[("gath", j)], (j, sp.v)

                        @blk.tensor
                        def _(tensor):
                            for j in range(T + 2):
                                if j >= 2:
                                    jj = j - 2
                                    tensor.wait_ge(sv.h, EV[("evacTT", jj)])
                                    if jj >= 1:
                                        tensor.wait_ge(ss.h, EV[("fmcopy", jj - 1)])
                                    m = None
                                    for c in range(NKC):
                                        m = tensor.transpose(psT[:, 128 * c:128 * (c + 1)],
                                                             atB[jj % 2][:, 128 * c:128 * (c + 1)], ident[:])
                                    stn.inc(m)
                                    assert stn.v == EV[("attnT", jj)]
                                if j <= T - 1:
                                    b = j % 2
                                    tensor.wait_ge(sp.h, EV[("gath", j)])
                                    if j >= 2:
                                        tensor.wait_ge(ss.h, EV[("fmcopy", j - 2)])
                                    m = None
                                    for c in range(NKC):
                                        m = tensor.transpose(psT[:, 128 * c:128 * (c + 1)],
                                                             Qsel[:, b, 128 * c:128 * (c + 1)], ident[:])
                                    stn.inc(m)
                                    assert stn.v == EV[("tq", j)]
                                    tensor.wait_ge(ss.h, EV[("evq", j)])
                                    m = None
                                    for c in range(NKC):
                                        m = tensor.transpose(psT[:, 128 * c:128 * (c + 1)],
                                                             Ksel[:, b, 128 * c:128 * (c + 1)], ident[:])
                                    stn.inc(m)
                                    assert stn.v == EV[("tk", j)]
                                    tensor.wait_ge(ss.h, EV[("evk", j)])
                                    if j >= 2:
                                        tensor.wait_ge(sv.h, EV[("subm", j - 2)])
                                    psAB = psA if b == 0 else psB
                                    m = None
                                    for parity in range(2):
                                        if parity == 1:
                                            stn.inc(m)
                                            tensor.wait_ge(stn.h, stn.v)
                                        pb = 64 * parity
                                        for h in range(parity, H, 2):
                                            cc = 128 * (h // 2)
                                            for half in range(2):
                                                m = tensor.matmul(
                                                    psAB[64 * half:64 * half + 64, 64 * h:64 * h + 64],
                                                    KselT[pb:pb + 64, cc + 64 * half:cc + 64 * half + 64],
                                                    QselT[pb:pb + 64, cc + 64 * half:cc + 64 * half + 64],
                                                    start=True, stop=True)
                                    stn.inc(m)
                                    assert stn.v == EV[("scores", j)]
                                if 1 <= j <= T:
                                    jj = j - 1
                                    tensor.wait_ge(ss.h, EV[("exp", jj)])
                                    if jj >= 2:
                                        tensor.wait_ge(sv.h, EV[("evacTT", jj - 2)])
                                        tensor.wait_ge(sv.h, EV[("radd", jj - 2)])
                                    m = None
                                    sx = sxB[jj % 2]
                                    for half in range(2):
                                        if half == 1:
                                            stn.inc(m)
                                            tensor.wait_ge(stn.h, stn.v)
                                        pb = 64 * half
                                        for h in range(H):
                                            po = psQ if h < 8 else psU
                                            co = 64 * h if h < 8 else 64 * (h - 8)
                                            tensor.matmul(po[pb:pb + 64, co:co + 64],
                                                          sx[pb:pb + 64, 64 * h:64 * h + 64],
                                                          Vsel[pb:pb + 64, jj % 2, 64 * h:64 * h + 64],
                                                          start=True, stop=True)
                                            m = tensor.matmul(psS[pb:pb + 64, 16 * (jj % 2) + h:16 * (jj % 2) + h + 1],
                                                              sx[pb:pb + 64, 64 * h:64 * h + 64],
                                                              onescol[pb:pb + 64, :],
                                                              start=True, stop=True)
                                    stn.inc(m)
                                    assert stn.v == EV[("pv", jj)]
                            assert stn.v == EV["end_cp"]

                        @blk.scalar
                        def _(scal):
                            for j in range(T + 2):
                                if j >= 2:
                                    jj = j - 2
                                    scal.wait_ge(stn.h, EV[("attnT", jj)])
                                    ss.inc(scal.copy(attn_fm[:, :, 128 * jj:128 * (jj + 1)],
                                                     psT[:].rearrange("p (c t) -> p c t", t=128)))
                                    assert ss.v == EV[("fmcopy", jj)]
                                if j <= T - 1:
                                    scal.wait_ge(stn.h, EV[("tq", j)])
                                    ss.inc(scal.copy(QselT[:], psT[:]))
                                    scal.wait_ge(stn.h, EV[("tk", j)])
                                    ss.inc(scal.copy(KselT[:], psT[:]))
                                    scal.wait_ge(sv.h, EV[("subm", j)])
                                    ss.inc(scal.activation(sxB[j % 2][:], sc4[:], AF.Exp, scale=SCALE))
                                    assert ss.v == EV[("exp", j)]
                            assert ss.v == EV["end_cs"]

                        @blk.vector
                        def _(vec):
                            for j in range(T + 2):
                                if j <= T - 1:
                                    b = j % 2
                                    psAB = psA if b == 0 else psB
                                    vec.wait_ge(stn.h, EV[("scores", j)])
                                    if j >= 1:
                                        vec.wait_ge(ss.h, EV[("exp", j - 1)])
                                    sv.inc(vec.tensor_tensor(
                                        sc3[:].rearrange("p (h q) -> p h q", q=64),
                                        psAB[:].rearrange("p (h q) -> p h q", q=64),
                                        maskU[:, 64 * j:64 * j + 64].rearrange("p q -> p () q").broadcast_to((128, 16, 64)),
                                        ALU.add))
                                    vec.wait_ge(sv.h, sv.v)
                                    sv.inc(vec.tensor_reduce(stats[:, 0:16],
                                                             sc3[:].rearrange("p (h q) -> p h q", q=64),
                                                             AX.X, ALU.max))
                                    vec.wait_ge(sv.h, sv.v)
                                    sv.inc(vec.scalar_tensor_tensor(
                                        sc4[:].rearrange("p (h q) -> p h q", q=64),
                                        sc3[:].rearrange("p (h q) -> p h q", q=64),
                                        uneg[:, j:j + 1],
                                        stats[:, 0:16].rearrange("p h -> p h ()").broadcast_to((128, 16, 64)),
                                        ALU.add, ALU.subtract))
                                    assert sv.v == EV[("subm", j)]
                                if 1 <= j <= T:
                                    jj = j - 1
                                    vec.wait_ge(stn.h, EV[("pv", jj)])
                                    sv.inc(vec.tensor_scalar_add(r_sb[:], psS[:, 16 * (jj % 2):16 * (jj % 2) + 16], 1e-6))
                                    vec.wait_ge(sv.h, sv.v)
                                    sv.inc(vec.reciprocal(r_sb[:], r_sb[:]))
                                    vec.wait_ge(sv.h, sv.v)
                                    at = atB[jj % 2]
                                    if jj >= 2:
                                        vec.wait_ge(stn.h, EV[("attnT", jj - 2)])
                                    sv.inc(vec.tensor_tensor(
                                        at[:, 0:512].rearrange("p (g d) -> p g d", d=64),
                                        psQ[:].rearrange("p (g d) -> p g d", d=64),
                                        r_sb[:, 0:8].rearrange("p h -> p h ()").broadcast_to((128, 8, 64)),
                                        ALU.mult))
                                    sv.inc(vec.tensor_tensor(
                                        at[:, 512:1024].rearrange("p (g d) -> p g d", d=64),
                                        psU[:].rearrange("p (g d) -> p g d", d=64),
                                        r_sb[:, 8:16].rearrange("p h -> p h ()").broadcast_to((128, 8, 64)),
                                        ALU.mult))
                                    assert sv.v == EV[("evacTT", jj)]
                            assert sv.v == EV["end_cv"]

                if stage >= 4:
                    T = NT
                    cbC = [sc0, sc1]
                    pbC = [sc2, sc3]
                    svC, stnC, ldC, lgC, spC = sv.v, stn.v, ld.v, lg.v, sp.v
                    projv = lambda t: stnC + 2 * (t + 1)
                    addsv_c = lambda t: svC + 2 * (t + 1)
                    ldcv = lambda t: ldC + 16 * (t + 1)

                    with nc.Block() as blk:
                        @blk.sync
                        def _(sync):
                            for t in range(T):
                                if t >= 1:
                                    sync.wait_ge(ld.h, ldcv(t - 1))
                                if t >= 2:
                                    sync.wait_ge(sv.h, addsv_c(t - 2))
                                sync.dma_start(out=cbC[t % 2][:],
                                               in_=xin[seq * SEQ + 128 * t:seq * SEQ + 128 * t + 128, :]).then_inc(ld.h, 16)
                                ld.v += 16

                        @blk.tensor
                        def _(tensor):
                            for t in range(T):
                                assert stn.v == stnC + 2 * t
                                if t >= 2:
                                    tensor.wait_ge(sv.h, addsv_c(t - 2))
                                banks = (psQ[:], psU[:]) if t % 2 == 0 else (psA[:, 0:512], psA[:, 512:1024])
                                for oc in range(2):
                                    for c in range(NKC):
                                        tensor.matmul(banks[oc], attn_fm[:, c, 128 * t:128 * (t + 1)],
                                                      Wproj_sb[:, c, 512 * oc:512 * (oc + 1)],
                                                      start=(c == 0), stop=False)
                                    stn.inc(tensor.matmul(banks[oc], ones_bf[0:1, :],
                                                          bproj_bf[0:1, 512 * oc:512 * (oc + 1)],
                                                          start=False, stop=True))

                        @blk.vector
                        def _(vec):
                            for t in range(T):
                                assert sv.v == svC + 2 * t
                                vec.wait_ge(stn.h, projv(t))
                                vec.wait_ge(ld.h, ldcv(t))
                                if t >= 2:
                                    vec.wait_ge(sp.h, spC + t - 1)
                                banks = (psQ[:], psU[:]) if t % 2 == 0 else (psA[:, 0:512], psA[:, 512:1024])
                                sv.inc(vec.tensor_add(pbC[t % 2][:, 0:512], banks[0], cbC[t % 2][:, 0:512]))
                                sv.inc(vec.tensor_add(pbC[t % 2][:, 512:1024], banks[1], cbC[t % 2][:, 512:1024]))

                        @blk.gpsimd
                        def _(gpsimd):
                            for t in range(1, T + 1):
                                gpsimd.wait_ge(sv.h, addsv_c(t - 1))
                                gpsimd.dma_start(
                                    out=partial_dram[seq * SEQ + 128 * (t - 1):seq * SEQ + 128 * t, :],
                                    in_=pbC[(t - 1) % 2][:]).then_inc(lg.h, 16)
                                lg.v += 16
                                gpsimd.wait_ge(lg.h, lg.v)
                                gpsimd.sem_inc(sp.h, 1)
                                sp.v += 1

        with ExitStack() as g2:
            g2sb = lambda n, s, d: g2.enter_context(nc.sbuf_tensor(n, s, d))
            wgqT = g2sb("wgqT", [128, NKC, FI], BF16)
            wuqT = g2sb("wuqT", [128, NKC, FI], BF16)
            wdqT = g2sb("wdqT", [128, NIC, D], BF16)
            hq2 = g2sb("hq2", [128, 2, FI], BF16)
            hqT = g2sb("hqT", [128, FI], BF16)
            g2a = ExitStack()
            wrow = g2a.enter_context(nc.sbuf_tensor("wrow", [128, 1024], F32))
            wq_bf = g2a.enter_context(nc.sbuf_tensor("wq_bf", [128, 1024], BF16))
            sgnW = g2a.enter_context(nc.sbuf_tensor("sgnW", [128, 1024], BF16))
            s1W = g2a.enter_context(nc.sbuf_tensor("s1W", [128, 1024], BF16))
            s2W = g2a.enter_context(nc.sbuf_tensor("s2W", [128, 1024], BF16))
            s3W = g2a.enter_context(nc.sbuf_tensor("s3W", [128, 1024], BF16))

            wrow2 = g2a.enter_context(nc.sbuf_tensor("wrow2", [128, 1024], F32))
            wq_bf2 = g2a.enter_context(nc.sbuf_tensor("wq_bf2", [128, 1024], BF16))

            if stage >= 6:
                chl = []
                for wsrc, wdst, nrch, ncols in [(wg, wgqT, NIC, D), (wu, wuqT, NIC, D), (wd, wdqT, NKC, FI)]:
                    c0 = 0
                    cols = []
                    while c0 < ncols:
                        cols.append((c0, min(1024, ncols - c0)))
                        c0 += cols[-1][1]
                    for r in range(nrch):
                        for (c0, cw) in cols:
                            chl.append((wsrc, wdst, r, c0, cw))
                NCH = len(chl)
                wrW = [wrow, wrow2]
                wqW = [wq_bf, wq_bf2]
                vWb = [sc3, sc0]
                a2Wb = [sc4, sc6]
                tabWb = [sc1, sc2]
                svW, ssW, stnW, ldW, sqW = sv.v, ss.v, stn.v, ld.v, sq.v
                EVW = {}
                cs, cv, cp, cq = ssW, svW, stnW, sqW
                for c in range(NCH + 1):
                    if c <= NCH - 1:
                        cv += QPRE_INCS
                        EVW[("v", c)] = cv
                    if c >= 1:
                        cv += QPOST_U
                        EVW[("post", c - 1)] = cv
                        EVW[("L3", c - 1)] = cv - 3
                        EVW[("comb", c - 1)] = cv - 2
                        EVW[("q1", c - 1)] = cv - 1
                    if c <= NCH - 1:
                        cs += 1
                        EVW[("a2", c)] = cs
                        cs += 3
                        EVW[("s3", c)] = cs
                    if c >= 1:
                        cs += 1
                        EVW[("sgn", c - 1)] = cs
                        cs += chl[c - 1][4] // 128
                        EVW[("e", c - 1)] = cs
                    if c <= NCH - 1:
                        cq += QPL_INCS
                        EVW[("pool", c)] = cq
                    if c <= NCH - 1:
                        cp += 1
                        EVW[("t", c)] = cp
                ldwv = lambda c: ldW + 16 * (c + 1)

                with nc.Block() as blk:
                    @blk.sync
                    def _(sync):
                        for c in range(NCH):
                            if c >= 1:
                                sync.wait_ge(ld.h, ldwv(c - 1))
                            if c >= 2:
                                sync.wait_ge(sv.h, EVW[("v", c - 2)])
                            wsrc, wdst, r, c0, cw = chl[c]
                            sync.dma_start(out=wrW[c % 2][:, 0:cw],
                                           in_=wsrc[128 * r:128 * (r + 1), c0:c0 + cw]).then_inc(ld.h, 16)
                            ld.v += 16

                    @blk.vector
                    def _(vec):
                        for c in range(NCH + 1):
                            if c <= NCH - 1:
                                vec.wait_ge(ld.h, ldwv(c))
                                if c >= 2:
                                    vec.wait_ge(ss.h, EVW[("sgn", c - 2)])
                                wsrc, wdst, r, c0, cw = chl[c]
                                _q_pre(vec, wrW[c % 2][:, 0:cw], cw // 16, bstats,
                                       88 * (c % 2), vWb[c % 2], sv)
                                assert sv.v == EVW[("v", c)]
                            if c >= 1:
                                b = c - 1
                                wsrc, wdst, r, c0, cw = chl[b]
                                vec.wait_ge(ss.h, EVW[("s3", b)])
                                vec.wait_ge(sq.h, EVW[("pool", b)])
                                if b >= 2:
                                    vec.wait_ge(stn.h, EVW[("t", b - 2)])
                                _q_post(vec, cw // 16, bstats, 88 * (b % 2), a2Wb[b % 2],
                                        sgnW, tabWb[b % 2], sc5,
                                        wqW[b % 2][:, 0:cw], sv,
                                        sgn_wait=(ss.h, EVW[("sgn", b)]))
                                assert sv.v == EVW[("post", b)]

                    @blk.gpsimd
                    def _(g):
                        for c in range(NCH):
                            g.wait_ge(ss.h, EVW[("s3", c)])
                            if c >= 2:
                                g.wait_ge(sv.h, EVW[("comb", c - 2)])
                            wsrc, wdst, r, c0, cw = chl[c]
                            _q_pool(g, s1W, s2W, s3W, tabWb[c % 2], cw, sq)
                            assert sq.v == EVW[("pool", c)]

                    @blk.tensor
                    def _(tensor):
                        for c in range(NCH):
                            assert stn.v == stnW + c
                            tensor.wait_ge(sv.h, EVW[("post", c)])
                            if c >= 1:
                                tensor.wait_ge(ss.h, EVW[("e", c - 1)])
                            wsrc, wdst, r, c0, cw = chl[c]
                            m = None
                            for k in range(cw // 128):
                                m = tensor.transpose(psT[:, 128 * k:128 * (k + 1)],
                                                     wqW[c % 2][:, 128 * k:128 * (k + 1)], ident[:])
                            stn.inc(m)

                    @blk.scalar
                    def _(scal):
                        for c in range(NCH + 1):
                            if c <= NCH - 1:
                                wsrc, wdst, r, c0, cw = chl[c]
                                scal.wait_ge(sv.h, EVW[("v", c)])
                                if c >= 2:
                                    scal.wait_ge(sv.h, EVW[("L3", c - 2)])
                                _q_scal(scal, vWb[c % 2][:, 0:cw], a2Wb[c % 2][:, 0:cw],
                                        s1W[:, 0:cw], s2W[:, 0:cw], s3W[:, 0:cw], qbias, ss,
                                        s_wait=(sq.h, EVW[("pool", c - 1)] - 2) if c >= 1 else None,
                                        s3_wait=(sq.h, EVW[("pool", c - 1)]) if c >= 1 else None)
                                assert ss.v == EVW[("s3", c)]
                            if c >= 1:
                                b = c - 1
                                wsrc, wdst, r, c0, cw = chl[b]
                                if b >= 1:
                                    scal.wait_ge(sv.h, EVW[("q1", b - 1)])
                                ss.inc(scal.activation(sgnW[:, 0:cw], vWb[b % 2][:, 0:cw], AF.Sign))
                                assert ss.v == EVW[("sgn", b)]
                                scal.wait_ge(stn.h, EVW[("t", b)])
                                wsrc, wdst, r, c0, cw = chl[b]
                                for k in range(cw // 128):
                                    kk = c0 // 128 + k
                                    ss.inc(scal.copy(wdst[:, kk, 128 * r:128 * (r + 1)],
                                                     psT[:, 128 * k:128 * (k + 1)]))
                                assert ss.v == EVW[("e", b)]

            g2a.close()
            pairbuf = g2sb("pairbuf", [128, 2, 2048], F32)
            import os as _os
            if _os.environ.get('KPRINT'):
                print('sbuf free bytes/partition during D:', nc.sbuf_bytes_remaining)

            nic_chunks = [(0, 512), (512, 512), (1024, 512), (1536, 512), (2048, 512), (2560, 256)]
            pair_cols = [(0, 1024), (1024, 1024), (2048, 768)]
            NTOT = NSEQ * NT if stage >= 7 else 0
            xqb = [sexp, attn_tm]
            pp = [sc0, sc1]
            rows = [(s * SEQ + 128 * tt) for s in range(NSEQ) for tt in range(NT)]
            svD, ssD, stnD, ldD, soutD, spD = sv.v, ss.v, stn.v, ld.v, sout.v, sp.v

            def cpf(gt):
                v = stnD
                for g in range(gt):
                    v += (3 if g <= NTOT - 1 else 0) + (4 if g >= 1 else 0)
                return v

            def csf(gt):
                v = ssD
                for g in range(gt):
                    v += (12 if g <= NTOT - 1 else 0) + (3 if g >= 1 else 0)
                return v

            def cvf(gt):
                v = svD
                for g in range(gt):
                    v += (63 if g <= NTOT - 1 else 0) + (2 if g >= 1 else 0)
                return v

            def pairv(gt, pq):
                return cpf(gt) + pq + 1

            def tgroup(gtm1, g):
                gt = gtm1 + 1
                return cpf(gt) + (3 if gt <= NTOT - 1 else 0) + g + 1

            def down_done(gtm1):
                return tgroup(gtm1, 2) + 1

            def spair(gt, pq, k):
                return csf(gt) + 4 * pq + k + 1

            def evacv(gtm1, g):
                gt = gtm1 + 1
                return csf(gt) + (12 if gt <= NTOT - 1 else 0) + g + 1

            def hmultv(gt, ci):
                pq, k = ci // 2, ci % 2
                return cvf(gt) + 21 * pq + k + 1

            def quantdonev(gt):
                return cvf(gt) + 63

            def addsv(gtm1):
                return cvf(gtm1 + 1) + 63 * (1 if gtm1 + 1 <= NTOT - 1 else 0) + 2

            def ldv(gt):
                return ldD + 32 * (gt + 1)

            gbank = [psQ, psB]
            if NTOT:
                with nc.Block() as blk:
                    @blk.sync
                    def _(sync):
                        for gt in range(NTOT):
                            if gt >= 1:
                                sync.wait_ge(ld.h, ldv(gt - 1))
                            if gt >= 2:
                                sync.wait_ge(sp.h, spD + gt - 1)
                                sync.wait_ge(stn.h, pairv(gt - 2, 2))
                            sync.dma_start(
                                out=xqb[gt % 2][:].rearrange("p (c t) -> p c t", t=128),
                                in_=xqT_dram[gt // NT, :, :, 128 * (gt % NT):128 * (gt % NT) + 128].rearrange("c p t -> p c t")).then_inc(ld.h, 16)
                            sync.dma_start(out=pp[gt % 2][:],
                                           in_=partial_dram[rows[gt]:rows[gt] + 128, :]).then_inc(ld.h, 16)
                            ld.v += 32

                    @blk.tensor
                    def _(tensor):
                        for gt in range(NTOT + 1):
                            assert stn.v == cpf(gt), (gt, stn.v, cpf(gt))
                            if gt <= NTOT - 1:
                                tensor.wait_ge(ld.h, ldv(gt))
                                for pq in range(3):
                                    m = None
                                    for sub in range(4):
                                        if pq >= 1:
                                            tensor.wait_ge(ss.h, spair(gt, pq - 1, sub))
                                        elif gt >= 1:
                                            tensor.wait_ge(ss.h, spair(gt - 1, 2, sub))
                                        k = sub % 2
                                        ci = 2 * pq + k
                                        i0, iw = nic_chunks[ci]
                                        if sub < 2:
                                            dst, w_src = gbank[k][:, 0:iw], wgqT
                                        else:
                                            dst = psU[:, 0:iw] if k == 0 else psB[:, 512:512 + iw]
                                            w_src = wuqT
                                        for c in range(NKC):
                                            m = tensor.matmul(dst, xqb[gt % 2][:, 128 * c:128 * (c + 1)],
                                                              w_src[:, c, i0:i0 + iw],
                                                              start=(c == 0), stop=(c == NKC - 1))
                                    stn.inc(m)
                                    assert stn.v == pairv(gt, pq)
                            if gt >= 1:
                                tensor.wait_ge(sv.h, quantdonev(gt - 1))
                                for g in range(3):
                                    if g == 0 and gt >= 2:
                                        tensor.wait_ge(ss.h, evacv(gt - 2, 2))
                                    if g > 0:
                                        tensor.wait_ge(ss.h, evacv(gt - 1, g - 1))
                                    ncg = min(8, NIC - 8 * g)
                                    m = None
                                    for c in range(ncg):
                                        m = tensor.transpose(
                                            psT[:, 128 * c:128 * (c + 1)],
                                            hq2[:, (gt - 1) % 2, 128 * (8 * g + c):128 * (8 * g + c + 1)],
                                            ident[:])
                                    stn.inc(m)
                                tensor.wait_ge(ss.h, evacv(gt - 1, 2))
                                if gt >= 2:
                                    tensor.wait_ge(sv.h, addsv(gt - 2))
                                m = None
                                for oc in range(2):
                                    for c in range(NIC):
                                        m = tensor.matmul(psA[:, 512 * oc:512 * (oc + 1)],
                                                          hqT[:, 128 * c:128 * (c + 1)],
                                                          wdqT[:, c, 512 * oc:512 * (oc + 1)],
                                                          start=(c == 0), stop=(c == NIC - 1))
                                stn.inc(m)
                                assert stn.v == down_done(gt - 1)

                    @blk.scalar
                    def _(scal):
                        for gt in range(NTOT + 1):
                            assert ss.v == csf(gt), (gt, ss.v, csf(gt))
                            if gt <= NTOT - 1:
                                for pq in range(3):
                                    scal.wait_ge(stn.h, pairv(gt, pq))
                                    if gt * 3 + pq >= 2:
                                        ppq = gt * 3 + pq - 2
                                        scal.wait_ge(sv.h, hmultv(ppq // 3, 2 * (ppq % 3) + 1))
                                    pb2 = pairbuf[:, (gt * 3 + pq) % 2, :]
                                    iwA = nic_chunks[2 * pq][1]
                                    iwB = nic_chunks[2 * pq + 1][1]
                                    gbA = gbank[0]
                                    gbB = gbank[1]
                                    ss.inc(scal.activation(pb2[:, 0:iwA], gbA[:, 0:iwA], AF.Silu))
                                    ss.inc(scal.activation(pb2[:, 512:512 + iwB], gbB[:, 0:iwB], AF.Silu))
                                    ss.inc(scal.copy(pb2[:, 1024:1024 + iwA], psU[:, 0:iwA]))
                                    ss.inc(scal.copy(pb2[:, 1536:1536 + iwB], psB[:, 512:512 + iwB]))
                            if gt >= 1:
                                for g in range(3):
                                    scal.wait_ge(stn.h, tgroup(gt - 1, g))
                                    ncg = min(8, NIC - 8 * g)
                                    ss.inc(scal.copy(hqT[:, 128 * 8 * g:128 * (8 * g + ncg)],
                                                     psT[:, 0:128 * ncg]))

                    @blk.vector
                    def _(vec):
                        for gt in range(NTOT + 1):
                            assert sv.v == cvf(gt), (gt, sv.v, cvf(gt))
                            if gt <= NTOT - 1:
                                if gt >= 2:
                                    vec.wait_ge(stn.h, tgroup(gt - 2, 2))
                                for pq in range(3):
                                    pb2 = pairbuf[:, (gt * 3 + pq) % 2, :]
                                    iwA = nic_chunks[2 * pq][1]
                                    iwB = nic_chunks[2 * pq + 1][1]
                                    vec.wait_ge(ss.h, spair(gt, pq, 2))
                                    sv.inc(vec.tensor_mul(sc2[:, 0:iwA], pb2[:, 0:iwA], pb2[:, 1024:1024 + iwA]))
                                    vec.wait_ge(ss.h, spair(gt, pq, 3))
                                    sv.inc(vec.tensor_mul(sc2[:, iwA:iwA + iwB], pb2[:, 512:512 + iwB],
                                                          pb2[:, 1536:1536 + iwB]))
                                    pw = iwA + iwB
                                    _emit_quant(vec, sc2[:, 0:pw], pw // 16, bstats,
                                                sc3, sc4, sc5, sc6,
                                                hq2[:, gt % 2, pair_cols[pq][0]:pair_cols[pq][0] + pw], sv)
                            if gt >= 1:
                                vec.wait_ge(stn.h, down_done(gt - 1))
                                vec.wait_ge(ld.h, ldv(gt - 1))
                                b = (gt - 1) % 2
                                sv.inc(vec.tensor_add(pp[b][:, 0:512], psA[:, 0:512], pp[b][:, 0:512]))
                                sv.inc(vec.tensor_add(pp[b][:, 512:1024], psA[:, 512:1024], pp[b][:, 512:1024]))
                                assert sv.v == addsv(gt - 1)

                    @blk.gpsimd
                    def _(gpsimd):
                        for gt in range(1, NTOT + 1):
                            gpsimd.wait_ge(sv.h, addsv(gt - 1))
                            gpsimd.dma_start(out=out[rows[gt - 1]:rows[gt - 1] + 128, :],
                                             in_=pp[(gt - 1) % 2][:]).then_inc(sout.h, 16)
                            sout.v += 16
                            gpsimd.wait_ge(sout.h, soutD + 16 * gt)
                            gpsimd.sem_inc(sp.h, 1)
                            sp.v += 1

            with nc.Block() as blk:
                @blk.gpsimd
                def _(gpsimd):
                    gpsimd.wait_ge(sout.h, sout.v)

    return nc


def _ssel():
    s = np.zeros((128, 128), dtype=ml_dtypes.bfloat16)
    s[0, 0:64] = 1.0
    s[64, 64:128] = 1.0
    return s


def kernel(L0_attn_x, L0_attn_Wqkv, L0_attn_bqkv, L0_attn_Wproj, L0_attn_bproj,
           L0_attn_mapping, L0_ffn_w_gate, L0_ffn_w_up, L0_ffn_w_down):
    x = np.ascontiguousarray(np.asarray(L0_attn_x, dtype=np.float32).reshape(16 * SEQ, D))
    consts = dict(
        Wqkv=np.ascontiguousarray(np.asarray(L0_attn_Wqkv, np.float32)),
        bqkv=np.ascontiguousarray(np.asarray(L0_attn_bqkv, np.float32).reshape(1, I3)),
        Wproj=np.ascontiguousarray(np.asarray(L0_attn_Wproj, np.float32)),
        bproj=np.ascontiguousarray(np.asarray(L0_attn_bproj, np.float32).reshape(1, D)),
        mapping=np.ascontiguousarray(np.asarray(L0_attn_mapping, np.int32).reshape(4096, 1)),
        wg=np.ascontiguousarray(np.asarray(L0_ffn_w_gate, np.float32)),
        wu=np.ascontiguousarray(np.asarray(L0_ffn_w_up, np.float32)),
        wd=np.ascontiguousarray(np.asarray(L0_ffn_w_down, np.float32)),
        ident_in=np.eye(128, dtype=ml_dtypes.bfloat16),
        ones_in=np.ones((1, 128), dtype=ml_dtypes.bfloat16),
        etop_in=np.concatenate([np.ones((1, 64)), np.zeros((1, 64))], axis=1).astype(ml_dtypes.bfloat16),
        ebot_in=np.concatenate([np.zeros((1, 64)), np.ones((1, 64))], axis=1).astype(ml_dtypes.bfloat16),
        ssel_in=_ssel(),
    )
    nc = build_nc()
    in_maps = []
    for c in range(8):
        m = dict(consts)
        m["xin"] = np.ascontiguousarray(x[c * NSEQ * SEQ:(c + 1) * NSEQ * SEQ, :])
        in_maps.append(m)
    res = run_bass_kernel_spmd(nc, in_maps, core_ids=list(range(8)))
    outs = [np.asarray(res.results[c]["out"]) for c in range(8)]
    return np.concatenate(outs, axis=0).reshape(16, SEQ, D)


# version 15
# speedup vs baseline: 2.4278x; 1.0068x over previous
import numpy as np
import ml_dtypes
import concourse.bass as bass
import concourse.mybir as mybir
from concourse.bass_utils import run_bass_kernel_spmd
from concourse.bass_core_rust import IndirectOffsetOnAxis
from contextlib import ExitStack

F32 = mybir.dt.float32
BF16 = mybir.dt.bfloat16
I32 = mybir.dt.int32
AF = mybir.ActivationFunctionType
ALU = mybir.AluOpType
AX = mybir.AxisListType

SEQ = 2048
D = 1024
NT = SEQ // 128
NSEQ = 2
H = 16
I3 = 3072
FI = 2816
NKC = D // 128
NIC = FI // 128
SCALE = 0.125

TL = [0.0625, 0.5625, 1.5625, 3.0625]
TSQ = [6.25, 12.25, 25.0]
QUANT_INCS = 19
QPRE_INCS = 5
QPOST_U = 7
QSC_INCS = 4
QPL_INCS = 3


class Sem:
    def __init__(self, handle):
        self.h = handle
        self.v = 0

    def inc(self, inst, n=1):
        inst.then_inc(self.h, n)
        self.v += n
        return self.v


def _emit_quant(vec, x_ap, nblk, bstats, v, a2, L, S, out_bf, sv):
    n = 16 * nblk
    sc = bstats[:, 0:nblk]
    rsc = bstats[:, 176:176 + nblk]
    hsc = bstats[:, 352:352 + nblk]
    seg = lambda ap: ap.rearrange("p (g s) -> p g s", s=16)
    bc = lambda ap: ap.rearrange("p g -> p g ()").broadcast_to((128, nblk, 16))
    vec.wait_ge(sv.h, sv.v)
    sv.inc(vec.tensor_reduce(sc, seg(x_ap), AX.X, ALU.max, apply_absolute_value=True))
    vec.wait_ge(sv.h, sv.v)
    sv.inc(vec.tensor_scalar(sc, sc, 1.0 / 6.0, 1e-12, ALU.mult, ALU.max))
    vec.wait_ge(sv.h, sv.v)
    sv.inc(vec.reciprocal(rsc, sc))
    sv.inc(vec.tensor_scalar_mul(hsc, sc, 0.5))
    vec.wait_ge(sv.h, sv.v)
    sv.inc(vec.tensor_tensor(seg(v[:, 0:n]), seg(x_ap), bc(rsc), ALU.mult))
    vec.wait_ge(sv.h, sv.v)
    sv.inc(vec.tensor_mul(a2[:, 0:n], v[:, 0:n], v[:, 0:n]))
    vec.wait_ge(sv.h, sv.v)
    sv.inc(vec.tensor_single_scalar(L[:, 0:n], a2[:, 0:n], TL[0], ALU.is_gt))
    for c in TL[1:]:
        vec.wait_ge(sv.h, sv.v)
        sv.inc(vec.scalar_tensor_tensor(L[:, 0:n], a2[:, 0:n], c, L[:, 0:n], ALU.is_gt, ALU.add))
    sv.inc(vec.tensor_single_scalar(S[:, 0:n], a2[:, 0:n], TSQ[2], ALU.is_gt))
    for c in (TSQ[1], TSQ[0], TSQ[2]):
        vec.wait_ge(sv.h, sv.v)
        sv.inc(vec.scalar_tensor_tensor(S[:, 0:n], a2[:, 0:n], c, S[:, 0:n], ALU.is_gt, ALU.add))
    vec.wait_ge(sv.h, sv.v)
    sv.inc(vec.scalar_tensor_tensor(L[:, 0:n], S[:, 0:n], 2.0, L[:, 0:n], ALU.mult, ALU.add))
    sv.inc(vec.tensor_single_scalar(a2[:, 0:n], v[:, 0:n], 0.0, ALU.is_gt))
    vec.wait_ge(sv.h, sv.v)
    sv.inc(vec.tensor_scalar(a2[:, 0:n], a2[:, 0:n], 2.0, -1.0, ALU.mult, ALU.add))
    vec.wait_ge(sv.h, sv.v)
    sv.inc(vec.tensor_mul(L[:, 0:n], L[:, 0:n], a2[:, 0:n]))
    vec.wait_ge(sv.h, sv.v)
    sv.inc(vec.tensor_tensor(seg(out_bf), seg(L[:, 0:n]), bc(hsc), ALU.mult))


def _q_pre(vec, x_ap, nblk, bstats, boff, v, sv):
    n = 16 * nblk
    sc = bstats[:, boff:boff + nblk]
    rsc = bstats[:, 176 + boff:176 + boff + nblk]
    hsc = bstats[:, 352 + boff:352 + boff + nblk]
    seg = lambda ap: ap.rearrange("p (g s) -> p g s", s=16)
    bc = lambda ap: ap.rearrange("p g -> p g ()").broadcast_to((128, nblk, 16))
    vec.wait_ge(sv.h, sv.v)
    sv.inc(vec.tensor_reduce(sc, seg(x_ap), AX.X, ALU.max, apply_absolute_value=True))
    vec.wait_ge(sv.h, sv.v)
    sv.inc(vec.tensor_scalar(sc, sc, 1.0 / 6.0, 1e-12, ALU.mult, ALU.max))
    vec.wait_ge(sv.h, sv.v)
    sv.inc(vec.reciprocal(rsc, sc))
    sv.inc(vec.tensor_scalar_mul(hsc, sc, 0.5))
    vec.wait_ge(sv.h, sv.v)
    sv.inc(vec.tensor_tensor(seg(v[:, 0:n]), seg(x_ap), bc(rsc), ALU.mult))


def _q_scal(scal, v_ap, a2_ap, s1_ap, s2_ap, s3_ap, qbias, ss,
            s_wait=None, s3_wait=None):
    ss.inc(scal.activation(a2_ap, v_ap, AF.Square))
    scal.wait_ge(ss.h, ss.v)
    if s_wait is not None:
        scal.wait_ge(s_wait[0], s_wait[1])
    ss.inc(scal.activation(s1_ap, a2_ap, AF.Sign, bias=qbias[:, 0:1]))
    ss.inc(scal.activation(s2_ap, a2_ap, AF.Sign, bias=qbias[:, 1:2]))
    if s3_wait is not None:
        scal.wait_ge(s3_wait[0], s3_wait[1])
    ss.inc(scal.activation(s3_ap, a2_ap, AF.Sign, bias=qbias[:, 2:3]))


def _q_pool(g, s1, s2, s3, u, n, sq):
    sq.inc(g.tensor_tensor(u[:, 0:n], s1[:, 0:n], s2[:, 0:n], ALU.add))
    g.wait_ge(sq.h, sq.v)
    sq.inc(g.tensor_tensor(u[:, 0:n], u[:, 0:n], s3[:, 0:n], ALU.add))
    g.wait_ge(sq.h, sq.v)
    sq.inc(g.tensor_tensor(u[:, 0:n], u[:, 0:n], s3[:, 0:n], ALU.add))
    g.wait_ge(sq.h, sq.v)


def _q_post(vec, nblk, bstats, boff, a2, sgn, u, L, out_bf, sv, sgn_wait=None,
            s123=None):
    n = 16 * nblk
    hsc = bstats[:, 352 + boff:352 + boff + nblk]
    seg = lambda ap: ap.rearrange("p (g s) -> p g s", s=16)
    bc = lambda ap: ap.rearrange("p g -> p g ()").broadcast_to((128, nblk, 16))
    vec.wait_ge(sv.h, sv.v)
    sv.inc(vec.tensor_scalar(L[:, 0:n], a2[:, 0:n], TL[0], 4.0, ALU.is_gt, ALU.add))
    for c in TL[1:]:
        vec.wait_ge(sv.h, sv.v)
        sv.inc(vec.scalar_tensor_tensor(L[:, 0:n], a2[:, 0:n], c, L[:, 0:n], ALU.is_gt, ALU.add))
    if u is not None:
        vec.wait_ge(sv.h, sv.v)
        sv.inc(vec.scalar_tensor_tensor(L[:, 0:n], u[:, 0:n], 0.0, L[:, 0:n], ALU.add, ALU.add))
    else:
        s1, s2, s3 = s123
        vec.wait_ge(sv.h, sv.v)
        sv.inc(vec.scalar_tensor_tensor(L[:, 0:n], s1[:, 0:n], 0.0, L[:, 0:n], ALU.add, ALU.add))
        vec.wait_ge(sv.h, sv.v)
        sv.inc(vec.scalar_tensor_tensor(L[:, 0:n], s2[:, 0:n], 0.0, L[:, 0:n], ALU.add, ALU.add))
        vec.wait_ge(sv.h, sv.v)
        sv.inc(vec.scalar_tensor_tensor(L[:, 0:n], s3[:, 0:n], 2.0, L[:, 0:n], ALU.mult, ALU.add))
    vec.wait_ge(sv.h, sv.v)
    if sgn_wait is not None:
        vec.wait_ge(sgn_wait[0], sgn_wait[1])
    sv.inc(vec.tensor_mul(L[:, 0:n], L[:, 0:n], sgn[:, 0:n]))
    vec.wait_ge(sv.h, sv.v)
    sv.inc(vec.tensor_tensor(seg(out_bf), seg(L[:, 0:n]), bc(hsc), ALU.mult))


def build_nc(stage=9):
    bsub = stage - 30 if 30 <= stage < 40 else 99
    nc = bass.Bass("TRN2", target_bir_lowering=False)

    xin = nc.dram_tensor("xin", [NSEQ * SEQ, D], F32, kind="ExternalInput").ap()
    Wqkv = nc.dram_tensor("Wqkv", [D, I3], F32, kind="ExternalInput").ap()
    bqkv = nc.dram_tensor("bqkv", [1, I3], F32, kind="ExternalInput").ap()
    Wproj = nc.dram_tensor("Wproj", [D, D], F32, kind="ExternalInput").ap()
    bproj = nc.dram_tensor("bproj", [1, D], F32, kind="ExternalInput").ap()
    mapping = nc.dram_tensor("mapping", [4096, 1], I32, kind="ExternalInput").ap()
    wg = nc.dram_tensor("wg", [FI, D], F32, kind="ExternalInput").ap()
    wu = nc.dram_tensor("wu", [FI, D], F32, kind="ExternalInput").ap()
    wd = nc.dram_tensor("wd", [D, FI], F32, kind="ExternalInput").ap()
    ident_in = nc.dram_tensor("ident_in", [128, 128], BF16, kind="ExternalInput").ap()
    ones_in = nc.dram_tensor("ones_in", [1, 128], BF16, kind="ExternalInput").ap()
    etop_in = nc.dram_tensor("etop_in", [1, 128], BF16, kind="ExternalInput").ap()
    ebot_in = nc.dram_tensor("ebot_in", [1, 128], BF16, kind="ExternalInput").ap()
    ssel_in = nc.dram_tensor("ssel_in", [128, 128], BF16, kind="ExternalInput").ap()

    out = nc.dram_tensor("out", [NSEQ * SEQ, D], F32, kind="ExternalOutput").ap()

    qkv_dram = nc.dram_tensor("qkv_dram", [SEQ, I3], BF16, kind="Internal").ap()
    partial_dram = nc.dram_tensor("partial_dram", [NSEQ * SEQ, D], F32, kind="Internal").ap()
    xqT_dram = nc.dram_tensor("xqT_dram", [NSEQ, NKC, 128, SEQ], BF16, kind="Internal").ap()
    kidx_dram = nc.dram_tensor("kidx_dram", [SEQ, 1], I32, kind="Internal").ap()

    with ExitStack() as st:
        sb = lambda n, s, d: st.enter_context(nc.sbuf_tensor(n, s, d))
        psb = lambda n, s, d: st.enter_context(nc.psum_tensor(n, s, d))
        mksem = lambda n: Sem(st.enter_context(nc.semaphore(n)))

        ld = mksem("ld")
        lg = mksem("lg")
        sv = mksem("sv")
        ss = mksem("ss")
        stn = mksem("stn")
        sout = mksem("sout")
        sp = mksem("sp")
        sq = mksem("sq")

        ident = sb("ident", [128, 128], BF16)
        ones_bf = sb("ones_bf", [1, 128], BF16)
        etop = sb("etop", [1, 128], BF16)
        ebot = sb("ebot", [1, 128], BF16)
        ssel = sb("ssel", [128, 128], BF16)
        onescol = sb("onescol", [128, 1], BF16)

        qidx = sb("qidx", [128, 16], I32)
        kidx = sb("kidx", [128, 16], I32)
        uneg = sb("uneg", [128, 16], F32)
        maskU = sb("maskU", [128, 1024], BF16)

        sc0 = sb("sc0", [128, 1024], F32)
        sc1 = sb("sc1", [128, 1024], F32)
        sc2 = sb("sc2", [128, 1024], F32)
        sc3 = sb("sc3", [128, 1024], F32)
        sc4 = sb("sc4", [128, 1024], F32)
        sc5 = sb("sc5", [128, 1024], F32)
        sc6 = sb("sc6", [128, 1024], F32)
        bstats = sb("bstats", [128, 528], F32)
        qbias = sb("qbias", [128, 3], F32)
        stats = sb("stats", [128, 16], F32)

        sexp = sb("sexp", [128, 1024], BF16)
        attn_tm = sb("attn_tm", [128, 1024], BF16)
        r_sb = sb("r_sb", [128, 16], F32)

        psQ = psb("psQ", [128, 512], F32)
        psU = psb("psU", [128, 512], F32)
        psA = psb("psA", [128, 1024], F32)
        psB = psb("psB", [128, 1024], F32)
        psT = psb("psT", [128, 1024], BF16)
        psS = psb("psS", [128, 32], F32)

        g0 = ExitStack()
        g0sb = lambda n, s, d: g0.enter_context(nc.sbuf_tensor(n, s, d))
        map_col = g0sb("map_col", [128, 16], I32)
        map_colf = g0sb("map_colf", [128, 16], F32)
        u_col = g0sb("u_col", [128, 16], F32)
        u_colbf = g0sb("u_colbf", [128, 16], BF16)
        m64 = g0sb("m64", [32, 1], I32)
        m64f = g0sb("m64f", [32, 4], F32)
        m64i = g0sb("m64i", [32, 1], I32)
        mseg = g0sb("mseg", [32, 256], I32)
        msegf = g0sb("msegf", [32, 256], F32)
        kidx32 = g0sb("kidx32", [32, 64], F32)
        kidx32i = g0sb("kidx32i", [32, 64], I32)
        map_row = g0sb("map_row", [1, 2048], I32)
        map_rowf = g0sb("map_rowf", [1, 2048], F32)
        mask_row = g0sb("mask_row", [1, 2048], BF16)
        mrow_a = g0sb("mrow_a", [1, 512], BF16)
        mrow_b = g0sb("mrow_b", [1, 512], BF16)
        with nc.Block() as blk:
            @blk.sync
            def _(sync):
                sync.dma_start(out=ident[:], in_=ident_in[:, :]).then_inc(ld.h, 16)
                sync.dma_start(out=ones_bf[:], in_=ones_in[:, :]).then_inc(ld.h, 16)
                sync.dma_start(out=etop[:], in_=etop_in[:, :]).then_inc(ld.h, 16)
                sync.dma_start(out=ebot[:], in_=ebot_in[:, :]).then_inc(ld.h, 16)
                sync.dma_start(out=ssel[:], in_=ssel_in[:, :]).then_inc(ld.h, 16)
                sync.dma_start(out=map_row[:],
                               in_=mapping[0:2048, 0].rearrange("(o t) -> o t", o=1)).then_inc(ld.h, 16)
                sync.dma_start(out=map_col[:],
                               in_=mapping[0:2048, 0].rearrange("(j p) -> p j", p=128),
                               allow_slow_non_contiguous=True).then_inc(ld.h, 16)
                sync.dma_start(out=m64[:],
                               in_=mapping[:, 0].rearrange("(n s) -> n s", s=64)[0:32, 0:1],
                               allow_slow_non_contiguous=True).then_inc(ld.h, 16)
                ld.v += 16 * 8

            @blk.vector
            def _(vec):
                vec.wait_ge(ld.h, ld.v)
                sv.inc(vec.memset(onescol[:], 1.0))
                sv.inc(vec.memset(qbias[:, 0:1], -TSQ[0]))
                sv.inc(vec.memset(qbias[:, 1:2], -TSQ[1]))
                sv.inc(vec.memset(qbias[:, 2:3], -TSQ[2]))
                sv.inc(vec.tensor_copy(map_colf[:], map_col[:]))
                vec.wait_ge(sv.h, sv.v)
                sv.inc(vec.tensor_scalar_min(map_colf[:], map_colf[:], 2047.0))
                vec.wait_ge(sv.h, sv.v)
                sv.inc(vec.tensor_copy(qidx[:], map_colf[:]))
                sv.inc(vec.tensor_copy(u_col[:], map_col[:]))
                vec.wait_ge(sv.h, sv.v)
                sv.inc(vec.tensor_single_scalar(u_col[:], u_col[:], 2048.0, ALU.is_lt))
                vec.wait_ge(sv.h, sv.v)
                sv.inc(vec.tensor_copy(u_colbf[:], u_col[:]))
                sv.inc(vec.tensor_copy(m64f[:, 0:1], m64[:]))
                vec.wait_ge(sv.h, sv.v)
                sv.inc(vec.tensor_single_scalar(m64f[:, 0:1], m64f[:, 0:1], 1.0 / 256.0, ALU.mult))
                vec.wait_ge(sv.h, sv.v)
                sv.inc(vec.tensor_copy(m64i[:], m64f[:, 0:1]))
                vec.wait_ge(sv.h, sv.v)
                sv.inc(vec.tensor_copy(m64f[:, 1:2], m64i[:]))
                vec.wait_ge(sv.h, sv.v)
                sv.inc(vec.tensor_tensor(m64f[:, 2:3], m64f[:, 1:2], m64f[:, 0:1], ALU.is_gt))
                vec.wait_ge(sv.h, sv.v)
                sv.inc(vec.tensor_sub(m64f[:, 3:4], m64f[:, 1:2], m64f[:, 2:3]))
                vec.wait_ge(sv.h, sv.v)
                sv.inc(vec.tensor_single_scalar(m64f[:, 3:4], m64f[:, 3:4], 256.0, ALU.mult))
                vec.wait_ge(sv.h, sv.v)
                sv.inc(vec.tensor_copy(m64i[:], m64f[:, 3:4]))
                sv.inc(vec.tensor_copy(map_rowf[:], map_row[:]))
                vec.wait_ge(sv.h, sv.v)
                sv.inc(vec.tensor_single_scalar(map_rowf[:], map_rowf[:], 2048.0, ALU.is_lt))
                vec.wait_ge(sv.h, sv.v)
                sv.inc(vec.tensor_scalar(map_rowf[:], map_rowf[:], -1.0, 8e9, ALU.add, ALU.mult))
                vec.wait_ge(sv.h, sv.v)
                sv.inc(vec.tensor_copy(mask_row[:], map_rowf[:]))

        with nc.Block() as blk:
            @blk.gpsimd
            def _(gpsimd):
                gpsimd.indirect_dma_start(
                    mseg[:], None, mapping[:, :],
                    IndirectOffsetOnAxis(m64i[:], 0)).then_inc(lg.h, 16)
                lg.v += 16
                gpsimd.wait_ge(lg.h, lg.v)
                gpsimd.sem_inc(sp.h, 1)
                sp.v += 1

            @blk.vector
            def _(vec):
                vec.wait_ge(sp.h, sp.v)
                sv.inc(vec.tensor_copy(msegf[:], mseg[:]))
                vec.wait_ge(sv.h, sv.v)
                sv.inc(vec.tensor_scalar_min(kidx32[:], msegf[:, 0:256:4], 2047.0))
                vec.wait_ge(sv.h, sv.v)
                sv.inc(vec.tensor_copy(kidx32i[:], kidx32[:]))

        with nc.Block() as blk:
            @blk.gpsimd
            def _(gpsimd):
                gpsimd.dma_start(out=kidx_dram[:, 0].rearrange("(p f) -> p f", p=32),
                                 in_=kidx32i[:]).then_inc(lg.h, 16)
                lg.v += 16
                gpsimd.wait_ge(lg.h, lg.v)
                gpsimd.dma_start(out=kidx[:],
                                 in_=kidx_dram[:, 0].rearrange("(j p) -> p j", p=128),
                                 allow_slow_non_contiguous=True).then_inc(lg.h, 16)
                lg.v += 16
                gpsimd.wait_ge(lg.h, lg.v)

            @blk.vector
            def _(vec):
                src = mask_row[0:1, :].rearrange("o (j q) -> o j q", q=128)
                sv.inc(vec.tensor_copy(mrow_a[0:1, :].rearrange("o (j q) -> o j q", q=64),
                                       src[:, 0:8, 0:64]))
                sv.inc(vec.tensor_copy(mrow_b[0:1, :].rearrange("o (j q) -> o j q", q=64),
                                       src[:, 0:8, 64:128]))

        with nc.Block() as blk:
            @blk.tensor
            def _(tensor):
                tensor.matmul(psA[:, 0:512], etop[0:1, :], mrow_a[0:1, :], start=True, stop=False)
                stn.inc(tensor.matmul(psA[:, 0:512], ebot[0:1, :], mrow_b[0:1, :], start=False, stop=True))
                stn.inc(tensor.matmul(psS[:, 0:16], ssel[:], u_colbf[:], start=True, stop=True))

            @blk.vector
            def _(vec):
                vec.wait_ge(stn.h, stn.v)
                sv.inc(vec.tensor_copy(maskU[:, 0:512], psA[:, 0:512]))
                sv.inc(vec.tensor_scalar(uneg[:], psS[:, 0:16], -1.0, 1e10, ALU.add, ALU.mult))
                src = mask_row[0:1, :].rearrange("o (j q) -> o j q", q=128)
                sv.inc(vec.tensor_copy(mrow_a[0:1, :].rearrange("o (j q) -> o j q", q=64),
                                       src[:, 8:16, 0:64]))
                sv.inc(vec.tensor_copy(mrow_b[0:1, :].rearrange("o (j q) -> o j q", q=64),
                                       src[:, 8:16, 64:128]))

        with nc.Block() as blk:
            @blk.tensor
            def _(tensor):
                tensor.matmul(psA[:, 512:1024], etop[0:1, :], mrow_a[0:1, :], start=True, stop=False)
                stn.inc(tensor.matmul(psA[:, 512:1024], ebot[0:1, :], mrow_b[0:1, :], start=False, stop=True))

            @blk.vector
            def _(vec):
                vec.wait_ge(stn.h, stn.v)
                sv.inc(vec.tensor_copy(maskU[:, 512:1024], psA[:, 512:1024]))

        g0.close()

        with ExitStack() as g1:
            g1sb = lambda n, s, d: g1.enter_context(nc.sbuf_tensor(n, s, d))
            xnT = g1sb("xnT", [128, 1024], BF16)
            xq_tm = g1sb("xq_tm", [128, 1024], BF16)
            Wqkv_sb = g1sb("Wqkv_sb", [128, NKC, I3], BF16)
            Wproj_sb = g1sb("Wproj_sb", [128, NKC, D], BF16)
            attn_fm = g1sb("attn_fm", [128, NKC, SEQ], BF16)
            Qsel = g1sb("Qsel", [128, 2, 1024], BF16)
            Ksel = g1sb("Ksel", [128, 2, 1024], BF16)
            Vsel = g1sb("Vsel", [128, 2, 1024], BF16)
            sexp2 = g1sb("sexp2", [128, 1024], BF16)
            attn_tm2 = g1sb("attn_tm2", [128, 1024], BF16)
            QselT = g1sb("QselT", [128, 1024], BF16)
            KselT = g1sb("KselT", [128, 1024], BF16)
            qkv_tile = g1sb("qkv_tile", [128, I3], BF16)
            qkv_tile2 = g1sb("qkv_tile2", [128, I3], BF16)
            xnT2 = g1sb("xnT2", [128, 1024], BF16)
            xq_tm2 = g1sb("xq_tm2", [128, 1024], BF16)
            xqTst = g1sb("xqTst", [128, 1024], BF16)
            xqTst2 = g1sb("xqTst2", [128, 1024], BF16)
            bqkv_bf = g1sb("bqkv_bf", [1, I3], BF16)
            bproj_bf = g1sb("bproj_bf", [1, D], BF16)

            with nc.Block() as blk:
                @blk.gpsimd
                def _(gpsimd):
                    gpsimd.dma_start(out=bqkv_bf[:], in_=bqkv[:, :]).then_inc(lg.h, 16)
                    gpsimd.dma_start(out=bproj_bf[:], in_=bproj[:, :]).then_inc(lg.h, 16)
                    for c in range(NKC):
                        gpsimd.dma_start(out=Wqkv_sb[:, c, :],
                                         in_=Wqkv[128 * c:128 * (c + 1), :]).then_inc(lg.h, 16)
                        gpsimd.dma_start(out=Wproj_sb[:, c, :],
                                         in_=Wproj[128 * c:128 * (c + 1), :]).then_inc(lg.h, 16)
                    lg.v += 16 * (2 + 2 * NKC)
                    gpsimd.wait_ge(lg.h, lg.v)

            for seq in range(NSEQ if stage >= 5 else (1 if (stage >= 2 or stage >= 30) else 0)):
                if True:
                    T = NT
                    xbA = [sc0, sc1]
                    xnbfA = [sexp, attn_tm]
                    xqA = [xq_tm, xq_tm2]
                    xnTA = [xnT, xnT2]
                    xqTA = [xqTst, xqTst2]
                    qkvA = [qkv_tile, qkv_tile2]
                    svA, ssA, stnA, ldA, lgA, spA = sv.v, ss.v, stn.v, ld.v, lg.v, sp.v

                    def cva(t):
                        return svA + 26 * t

                    def csa(t):
                        return ssA + (9 * t - 8 if t >= 1 else 0)

                    def cpa(t):
                        return stnA + (8 * t - 6 if t >= 1 else 0)

                    def sc_off(t):
                        return 1 if t <= T - 1 else 0

                    rstdv = lambda t: csa(t) + 1
                    xnTev = lambda tm1: csa(tm1 + 1) + sc_off(tm1 + 1) + 1
                    qkvev = lambda tm1, oc: csa(tm1 + 1) + sc_off(tm1 + 1) + 2 + oc
                    xqTev = lambda tm1: csa(tm1 + 1) + sc_off(tm1 + 1) + 8
                    qkvchunk = lambda tm1, oc: cpa(tm1 + 1) + oc + 1
                    txnv = lambda t: cpa(t) + (6 if t >= 1 else 0) + 1
                    txqv = lambda t: txnv(t) + 1
                    ldav = lambda t: ldA + 16 * (t + 1)

                    with nc.Block() as blk:
                        @blk.sync
                        def _(sync):
                            for t in range(T):
                                if t >= 1:
                                    sync.wait_ge(ld.h, ldav(t - 1))
                                if t >= 2:
                                    sync.wait_ge(sv.h, cva(t - 2) + 6)
                                sync.dma_start(out=xbA[t % 2][:],
                                               in_=xin[seq * SEQ + 128 * t:seq * SEQ + 128 * t + 128, :]).then_inc(ld.h, 16)
                                ld.v += 16

                        @blk.vector
                        def _(vec):
                            for t in range(T):
                                assert sv.v == cva(t), (t, sv.v)
                                vec.wait_ge(ld.h, ldav(t))
                                sv.inc(vec.bn_stats(stats[:, 0:6], xbA[t % 2][:, 0:512]))
                                sv.inc(vec.bn_stats(stats[:, 6:12], xbA[t % 2][:, 512:1024]))
                                vec.wait_ge(sv.h, sv.v)
                                sv.inc(vec.bn_aggr(stats[:, 12:14], stats[:, 0:12]))
                                vec.wait_ge(sv.h, sv.v)
                                sv.inc(vec.tensor_scalar_add(stats[:, 14:15], stats[:, 13:14], 1e-5))
                                vec.wait_ge(sv.h, sv.v)
                                sv.inc(vec.reciprocal(stats[:, 15:16], stats[:, 14:15]))
                                vec.wait_ge(ss.h, rstdv(t))
                                sv.inc(vec.tensor_scalar(sc2[:], xbA[t % 2][:], stats[:, 12:13], stats[:, 14:15],
                                                         ALU.subtract, ALU.mult))
                                if t >= 2:
                                    vec.wait_ge(stn.h, txnv(t - 2))
                                vec.wait_ge(sv.h, sv.v)
                                sv.inc(vec.tensor_copy(xnbfA[t % 2][:], sc2[:]))
                                if t >= 2:
                                    vec.wait_ge(stn.h, txqv(t - 2))
                                _emit_quant(vec, sc2[:], 64, bstats, sc3, sc4, sc5, sc6,
                                            xqA[t % 2][:], sv)

                        @blk.scalar
                        def _(scal):
                            for t in range(T + 1):
                                assert ss.v == csa(t), (t, ss.v)
                                if t <= T - 1:
                                    scal.wait_ge(sv.h, cva(t) + 5)
                                    ss.inc(scal.activation(stats[:, 14:15], stats[:, 15:16], AF.Sqrt))
                                if t >= 1:
                                    scal.wait_ge(stn.h, txnv(t - 1))
                                    ss.inc(scal.copy(xnTA[(t - 1) % 2][:], psT[:]))
                                    for oc in range(6):
                                        scal.wait_ge(stn.h, qkvchunk(t - 1, oc))
                                        if t >= 2 and oc == 0:
                                            scal.wait_ge(sp.h, spA + t - 1)
                                        pq = psQ if oc % 2 == 0 else psU
                                        ss.inc(scal.copy(qkvA[(t - 1) % 2][:, 512 * oc:512 * (oc + 1)],
                                                         pq[:]))
                                    scal.wait_ge(stn.h, txqv(t - 1))
                                    ss.inc(scal.copy(xqTA[(t - 1) % 2][:], psT[:]))

                        @blk.tensor
                        def _(tensor):
                            for t in range(T + 1):
                                assert stn.v == cpa(t), (t, stn.v)
                                if t >= 1:
                                    tensor.wait_ge(ss.h, xnTev(t - 1))
                                    for oc in range(6):
                                        if oc >= 2:
                                            tensor.wait_ge(ss.h, qkvev(t - 1, oc - 2))
                                        pq = psQ if oc % 2 == 0 else psU
                                        for c in range(NKC):
                                            tensor.matmul(pq[:], xnTA[(t - 1) % 2][:, 128 * c:128 * (c + 1)],
                                                          Wqkv_sb[:, c, 512 * oc:512 * (oc + 1)],
                                                          start=(c == 0), stop=False)
                                        stn.inc(tensor.matmul(pq[:], ones_bf[0:1, :],
                                                              bqkv_bf[0:1, 512 * oc:512 * (oc + 1)],
                                                              start=False, stop=True))
                                if t <= T - 1:
                                    vwait = cva(t) + 7
                                    tensor.wait_ge(sv.h, vwait)
                                    if t >= 1:
                                        tensor.wait_ge(ss.h, xqTev(t - 1))
                                    m = None
                                    for c in range(NKC):
                                        m = tensor.transpose(psT[:, 128 * c:128 * (c + 1)],
                                                             xnbfA[t % 2][:, 128 * c:128 * (c + 1)], ident[:])
                                    stn.inc(m)
                                    tensor.wait_ge(sv.h, cva(t + 1))
                                    tensor.wait_ge(ss.h, xnTev(t))
                                    m = None
                                    for c in range(NKC):
                                        m = tensor.transpose(psT[:, 128 * c:128 * (c + 1)],
                                                             xqA[t % 2][:, 128 * c:128 * (c + 1)], ident[:])
                                    stn.inc(m)

                        @blk.gpsimd
                        def _(gpsimd):
                            for t in range(1, T + 1):
                                gpsimd.wait_ge(ss.h, xqTev(t - 1))
                                gpsimd.dma_start(out=qkv_dram[128 * (t - 1):128 * t, :],
                                                 in_=qkvA[(t - 1) % 2][:]).then_inc(lg.h, 16)
                                gpsimd.dma_start(
                                    out=xqT_dram[seq, :, :, 128 * (t - 1):128 * t].rearrange("c p t -> p c t"),
                                    in_=xqTA[(t - 1) % 2][:].rearrange("p (c t) -> p c t", t=128)).then_inc(lg.h, 16)
                                lg.v += 32
                                gpsimd.wait_ge(lg.h, lg.v)
                                gpsimd.sem_inc(sp.h, 1)
                                sp.v += 1

                if stage >= 3 or stage >= 30:
                    T = NT
                    sxB = [sexp, sexp2]
                    atB = [attn_tm, attn_tm2]
                    svB, ssB, stnB, lgB, spB = sv.v, ss.v, stn.v, lg.v, sp.v

                    EV = {}

                    def prepass():
                        cs, cv, cp, csp = ssB, svB, stnB, spB
                        for j in range(T + 2):
                            if j <= T - 1:
                                csp += 1
                                EV[("gath", j)] = csp
                            if j >= 2:
                                cp += 1; EV[("attnT", j - 2)] = cp
                            if j <= T - 1:
                                cp += 1; EV[("tq", j)] = cp
                                cp += 1; EV[("tk", j)] = cp
                                cp += 2; EV[("scores", j)] = cp
                            if 1 <= j <= T:
                                cp += 2; EV[("pv", j - 1)] = cp
                            if j >= 2:
                                cs += 1; EV[("fmcopy", j - 2)] = cs
                            if j <= T - 1:
                                cs += 1; EV[("evq", j)] = cs
                                cs += 1; EV[("evk", j)] = cs
                                cs += 1; EV[("exp", j)] = cs
                            if j <= T - 1:
                                cv += 1; EV[("masked", j)] = cv
                                cv += 1; EV[("colmax", j)] = cv
                                cv += 1; EV[("subm", j)] = cv
                            if 1 <= j <= T:
                                cv += 1; EV[("radd", j - 1)] = cv
                                cv += 1; EV[("recip", j - 1)] = cv
                                cv += 2; EV[("evacTT", j - 1)] = cv
                        EV["end_cp"] = cp
                        EV["end_cs"] = cs
                        EV["end_cv"] = cv
                        EV["end_csp"] = csp
                    prepass()

                    with nc.Block() as blk:
                        @blk.gpsimd
                        def _(gpsimd):
                            for j in range(T):
                                if j >= 2:
                                    gpsimd.wait_ge(stn.h, EV[("pv", j - 2)])
                                b = j % 2
                                gpsimd.indirect_dma_start(
                                    Qsel[:, b, :], None, qkv_dram[:, :],
                                    IndirectOffsetOnAxis(qidx[:, j:j + 1], 0)).then_inc(lg.h, 16)
                                gpsimd.indirect_dma_start(
                                    Ksel[:, b, :], None, qkv_dram[:, :],
                                    IndirectOffsetOnAxis(kidx[:, j:j + 1], 0),
                                    element_offset=1024).then_inc(lg.h, 16)
                                gpsimd.indirect_dma_start(
                                    Vsel[:, b, :], None, qkv_dram[:, :],
                                    IndirectOffsetOnAxis(kidx[:, j:j + 1], 0),
                                    element_offset=2048).then_inc(lg.h, 16)
                                lg.v += 48
                                gpsimd.wait_ge(lg.h, lg.v)
                                gpsimd.sem_inc(sp.h, 1)
                                sp.v += 1
                                assert sp.v == EV[("gath", j)], (j, sp.v)

                        @blk.tensor
                        def _(tensor):
                            for j in range(T + 2):
                                if j >= 2:
                                    jj = j - 2
                                    tensor.wait_ge(sv.h, EV[("evacTT", jj)])
                                    if jj >= 1:
                                        tensor.wait_ge(ss.h, EV[("fmcopy", jj - 1)])
                                    m = None
                                    for c in range(NKC):
                                        m = tensor.transpose(psT[:, 128 * c:128 * (c + 1)],
                                                             atB[jj % 2][:, 128 * c:128 * (c + 1)], ident[:])
                                    stn.inc(m)
                                    assert stn.v == EV[("attnT", jj)]
                                if j <= T - 1:
                                    b = j % 2
                                    tensor.wait_ge(sp.h, EV[("gath", j)])
                                    if j >= 2:
                                        tensor.wait_ge(ss.h, EV[("fmcopy", j - 2)])
                                    m = None
                                    for c in range(NKC):
                                        m = tensor.transpose(psT[:, 128 * c:128 * (c + 1)],
                                                             Qsel[:, b, 128 * c:128 * (c + 1)], ident[:])
                                    stn.inc(m)
                                    assert stn.v == EV[("tq", j)]
                                    tensor.wait_ge(ss.h, EV[("evq", j)])
                                    m = None
                                    for c in range(NKC):
                                        m = tensor.transpose(psT[:, 128 * c:128 * (c + 1)],
                                                             Ksel[:, b, 128 * c:128 * (c + 1)], ident[:])
                                    stn.inc(m)
                                    assert stn.v == EV[("tk", j)]
                                    tensor.wait_ge(ss.h, EV[("evk", j)])
                                    if j >= 2:
                                        tensor.wait_ge(sv.h, EV[("subm", j - 2)])
                                    psAB = psA if b == 0 else psB
                                    m = None
                                    for parity in range(2):
                                        if parity == 1:
                                            stn.inc(m)
                                            tensor.wait_ge(stn.h, stn.v)
                                        pb = 64 * parity
                                        for h in range(parity, H, 2):
                                            cc = 128 * (h // 2)
                                            for half in range(2):
                                                m = tensor.matmul(
                                                    psAB[64 * half:64 * half + 64, 64 * h:64 * h + 64],
                                                    KselT[pb:pb + 64, cc + 64 * half:cc + 64 * half + 64],
                                                    QselT[pb:pb + 64, cc + 64 * half:cc + 64 * half + 64],
                                                    start=True, stop=True)
                                    stn.inc(m)
                                    assert stn.v == EV[("scores", j)]
                                if 1 <= j <= T:
                                    jj = j - 1
                                    tensor.wait_ge(ss.h, EV[("exp", jj)])
                                    if jj >= 2:
                                        tensor.wait_ge(sv.h, EV[("evacTT", jj - 2)])
                                        tensor.wait_ge(sv.h, EV[("radd", jj - 2)])
                                    m = None
                                    sx = sxB[jj % 2]
                                    for half in range(2):
                                        if half == 1:
                                            stn.inc(m)
                                            tensor.wait_ge(stn.h, stn.v)
                                        pb = 64 * half
                                        for h in range(H):
                                            po = psQ if h < 8 else psU
                                            co = 64 * h if h < 8 else 64 * (h - 8)
                                            tensor.matmul(po[pb:pb + 64, co:co + 64],
                                                          sx[pb:pb + 64, 64 * h:64 * h + 64],
                                                          Vsel[pb:pb + 64, jj % 2, 64 * h:64 * h + 64],
                                                          start=True, stop=True)
                                            m = tensor.matmul(psS[pb:pb + 64, 16 * (jj % 2) + h:16 * (jj % 2) + h + 1],
                                                              sx[pb:pb + 64, 64 * h:64 * h + 64],
                                                              onescol[pb:pb + 64, :],
                                                              start=True, stop=True)
                                    stn.inc(m)
                                    assert stn.v == EV[("pv", jj)]
                            assert stn.v == EV["end_cp"]

                        @blk.scalar
                        def _(scal):
                            for j in range(T + 2):
                                if j >= 2:
                                    jj = j - 2
                                    scal.wait_ge(stn.h, EV[("attnT", jj)])
                                    ss.inc(scal.copy(attn_fm[:, :, 128 * jj:128 * (jj + 1)],
                                                     psT[:].rearrange("p (c t) -> p c t", t=128)))
                                    assert ss.v == EV[("fmcopy", jj)]
                                if j <= T - 1:
                                    scal.wait_ge(stn.h, EV[("tq", j)])
                                    ss.inc(scal.copy(QselT[:], psT[:]))
                                    scal.wait_ge(stn.h, EV[("tk", j)])
                                    ss.inc(scal.copy(KselT[:], psT[:]))
                                    scal.wait_ge(sv.h, EV[("subm", j)])
                                    ss.inc(scal.activation(sxB[j % 2][:], sc4[:], AF.Exp, scale=SCALE))
                                    assert ss.v == EV[("exp", j)]
                            assert ss.v == EV["end_cs"]

                        @blk.vector
                        def _(vec):
                            for j in range(T + 2):
                                if j <= T - 1:
                                    b = j % 2
                                    psAB = psA if b == 0 else psB
                                    vec.wait_ge(stn.h, EV[("scores", j)])
                                    if j >= 1:
                                        vec.wait_ge(ss.h, EV[("exp", j - 1)])
                                    sv.inc(vec.tensor_tensor(
                                        sc3[:].rearrange("p (h q) -> p h q", q=64),
                                        psAB[:].rearrange("p (h q) -> p h q", q=64),
                                        maskU[:, 64 * j:64 * j + 64].rearrange("p q -> p () q").broadcast_to((128, 16, 64)),
                                        ALU.add))
                                    vec.wait_ge(sv.h, sv.v)
                                    sv.inc(vec.tensor_reduce(stats[:, 0:16],
                                                             sc3[:].rearrange("p (h q) -> p h q", q=64),
                                                             AX.X, ALU.max))
                                    vec.wait_ge(sv.h, sv.v)
                                    sv.inc(vec.scalar_tensor_tensor(
                                        sc4[:].rearrange("p (h q) -> p h q", q=64),
                                        sc3[:].rearrange("p (h q) -> p h q", q=64),
                                        uneg[:, j:j + 1],
                                        stats[:, 0:16].rearrange("p h -> p h ()").broadcast_to((128, 16, 64)),
                                        ALU.add, ALU.subtract))
                                    assert sv.v == EV[("subm", j)]
                                if 1 <= j <= T:
                                    jj = j - 1
                                    vec.wait_ge(stn.h, EV[("pv", jj)])
                                    sv.inc(vec.tensor_scalar_add(r_sb[:], psS[:, 16 * (jj % 2):16 * (jj % 2) + 16], 1e-6))
                                    vec.wait_ge(sv.h, sv.v)
                                    sv.inc(vec.reciprocal(r_sb[:], r_sb[:]))
                                    vec.wait_ge(sv.h, sv.v)
                                    at = atB[jj % 2]
                                    if jj >= 2:
                                        vec.wait_ge(stn.h, EV[("attnT", jj - 2)])
                                    sv.inc(vec.tensor_tensor(
                                        at[:, 0:512].rearrange("p (g d) -> p g d", d=64),
                                        psQ[:].rearrange("p (g d) -> p g d", d=64),
                                        r_sb[:, 0:8].rearrange("p h -> p h ()").broadcast_to((128, 8, 64)),
                                        ALU.mult))
                                    sv.inc(vec.tensor_tensor(
                                        at[:, 512:1024].rearrange("p (g d) -> p g d", d=64),
                                        psU[:].rearrange("p (g d) -> p g d", d=64),
                                        r_sb[:, 8:16].rearrange("p h -> p h ()").broadcast_to((128, 8, 64)),
                                        ALU.mult))
                                    assert sv.v == EV[("evacTT", jj)]
                            assert sv.v == EV["end_cv"]

                if stage >= 4:
                    T = NT
                    cbC = [sc0, sc1]
                    pbC = [sc2, sc3]
                    svC, stnC, ldC, lgC, spC = sv.v, stn.v, ld.v, lg.v, sp.v
                    projv = lambda t: stnC + 2 * (t + 1)
                    addsv_c = lambda t: svC + 2 * (t + 1)
                    ldcv = lambda t: ldC + 16 * (t + 1)

                    with nc.Block() as blk:
                        @blk.sync
                        def _(sync):
                            for t in range(T):
                                if t >= 1:
                                    sync.wait_ge(ld.h, ldcv(t - 1))
                                if t >= 2:
                                    sync.wait_ge(sv.h, addsv_c(t - 2))
                                sync.dma_start(out=cbC[t % 2][:],
                                               in_=xin[seq * SEQ + 128 * t:seq * SEQ + 128 * t + 128, :]).then_inc(ld.h, 16)
                                ld.v += 16

                        @blk.tensor
                        def _(tensor):
                            for t in range(T):
                                assert stn.v == stnC + 2 * t
                                if t >= 2:
                                    tensor.wait_ge(sv.h, addsv_c(t - 2))
                                banks = (psQ[:], psU[:]) if t % 2 == 0 else (psA[:, 0:512], psA[:, 512:1024])
                                for oc in range(2):
                                    for c in range(NKC):
                                        tensor.matmul(banks[oc], attn_fm[:, c, 128 * t:128 * (t + 1)],
                                                      Wproj_sb[:, c, 512 * oc:512 * (oc + 1)],
                                                      start=(c == 0), stop=False)
                                    stn.inc(tensor.matmul(banks[oc], ones_bf[0:1, :],
                                                          bproj_bf[0:1, 512 * oc:512 * (oc + 1)],
                                                          start=False, stop=True))

                        @blk.vector
                        def _(vec):
                            for t in range(T):
                                assert sv.v == svC + 2 * t
                                vec.wait_ge(stn.h, projv(t))
                                vec.wait_ge(ld.h, ldcv(t))
                                if t >= 2:
                                    vec.wait_ge(sp.h, spC + t - 1)
                                banks = (psQ[:], psU[:]) if t % 2 == 0 else (psA[:, 0:512], psA[:, 512:1024])
                                sv.inc(vec.tensor_add(pbC[t % 2][:, 0:512], banks[0], cbC[t % 2][:, 0:512]))
                                sv.inc(vec.tensor_add(pbC[t % 2][:, 512:1024], banks[1], cbC[t % 2][:, 512:1024]))

                        @blk.gpsimd
                        def _(gpsimd):
                            for t in range(1, T + 1):
                                gpsimd.wait_ge(sv.h, addsv_c(t - 1))
                                gpsimd.dma_start(
                                    out=partial_dram[seq * SEQ + 128 * (t - 1):seq * SEQ + 128 * t, :],
                                    in_=pbC[(t - 1) % 2][:]).then_inc(lg.h, 16)
                                lg.v += 16
                                gpsimd.wait_ge(lg.h, lg.v)
                                gpsimd.sem_inc(sp.h, 1)
                                sp.v += 1

        with ExitStack() as g2:
            g2sb = lambda n, s, d: g2.enter_context(nc.sbuf_tensor(n, s, d))
            wgqT = g2sb("wgqT", [128, NKC, FI], BF16)
            wuqT = g2sb("wuqT", [128, NKC, FI], BF16)
            wdqT = g2sb("wdqT", [128, NIC, D], BF16)
            hq2 = g2sb("hq2", [128, 2, FI], BF16)
            hqT = g2sb("hqT", [128, FI], BF16)
            g2a = ExitStack()
            wrow = g2a.enter_context(nc.sbuf_tensor("wrow", [128, 1024], F32))
            wq_bf = g2a.enter_context(nc.sbuf_tensor("wq_bf", [128, 1024], BF16))
            sgnW = g2a.enter_context(nc.sbuf_tensor("sgnW", [128, 1024], BF16))
            s1W = g2a.enter_context(nc.sbuf_tensor("s1W", [128, 1024], BF16))
            s2W = g2a.enter_context(nc.sbuf_tensor("s2W", [128, 1024], BF16))
            s3W = g2a.enter_context(nc.sbuf_tensor("s3W", [128, 1024], BF16))

            wrow2 = g2a.enter_context(nc.sbuf_tensor("wrow2", [128, 1024], F32))
            wq_bf2 = g2a.enter_context(nc.sbuf_tensor("wq_bf2", [128, 1024], BF16))

            if stage >= 6:
                chl = []
                for wsrc, wdst, nrch, ncols in [(wg, wgqT, NIC, D), (wu, wuqT, NIC, D), (wd, wdqT, NKC, FI)]:
                    c0 = 0
                    cols = []
                    while c0 < ncols:
                        cols.append((c0, min(1024, ncols - c0)))
                        c0 += cols[-1][1]
                    for r in range(nrch):
                        for (c0, cw) in cols:
                            chl.append((wsrc, wdst, r, c0, cw))
                NCH = len(chl)
                wrW = [wrow, wrow2]
                wqW = [wq_bf, wq_bf2]
                vWb = [sc3, sc0]
                a2Wb = [sc4, sc6]
                tabWb = [sc1, sc2]
                svW, ssW, stnW, ldW, sqW = sv.v, ss.v, stn.v, ld.v, sq.v
                EVW = {}
                cs, cv, cp, cq = ssW, svW, stnW, sqW
                for c in range(NCH + 1):
                    if c <= NCH - 1:
                        cv += QPRE_INCS
                        EVW[("v", c)] = cv
                    if c >= 1:
                        cv += QPOST_U
                        EVW[("post", c - 1)] = cv
                        EVW[("L3", c - 1)] = cv - 3
                        EVW[("comb", c - 1)] = cv - 2
                        EVW[("q1", c - 1)] = cv - 1
                    if c <= NCH - 1:
                        cs += 1
                        EVW[("a2", c)] = cs
                        cs += 3
                        EVW[("s3", c)] = cs
                    if c >= 1:
                        cs += 1
                        EVW[("sgn", c - 1)] = cs
                        cs += chl[c - 1][4] // 128
                        EVW[("e", c - 1)] = cs
                    if c <= NCH - 1:
                        cq += QPL_INCS
                        EVW[("pool", c)] = cq
                    if c <= NCH - 1:
                        cp += 1
                        EVW[("t", c)] = cp
                ldwv = lambda c: ldW + 16 * (c + 1)

                with nc.Block() as blk:
                    @blk.sync
                    def _(sync):
                        for c in range(NCH):
                            if c >= 1:
                                sync.wait_ge(ld.h, ldwv(c - 1))
                            if c >= 2:
                                sync.wait_ge(sv.h, EVW[("v", c - 2)])
                            wsrc, wdst, r, c0, cw = chl[c]
                            sync.dma_start(out=wrW[c % 2][:, 0:cw],
                                           in_=wsrc[128 * r:128 * (r + 1), c0:c0 + cw]).then_inc(ld.h, 16)
                            ld.v += 16

                    @blk.vector
                    def _(vec):
                        for c in range(NCH + 1):
                            if c <= NCH - 1:
                                vec.wait_ge(ld.h, ldwv(c))
                                if c >= 2:
                                    vec.wait_ge(ss.h, EVW[("sgn", c - 2)])
                                wsrc, wdst, r, c0, cw = chl[c]
                                _q_pre(vec, wrW[c % 2][:, 0:cw], cw // 16, bstats,
                                       88 * (c % 2), vWb[c % 2], sv)
                                assert sv.v == EVW[("v", c)]
                            if c >= 1:
                                b = c - 1
                                wsrc, wdst, r, c0, cw = chl[b]
                                vec.wait_ge(ss.h, EVW[("s3", b)])
                                vec.wait_ge(sq.h, EVW[("pool", b)])
                                if b >= 2:
                                    vec.wait_ge(stn.h, EVW[("t", b - 2)])
                                _q_post(vec, cw // 16, bstats, 88 * (b % 2), a2Wb[b % 2],
                                        sgnW, tabWb[b % 2], sc5,
                                        wqW[b % 2][:, 0:cw], sv,
                                        sgn_wait=(ss.h, EVW[("sgn", b)]))
                                assert sv.v == EVW[("post", b)]

                    @blk.gpsimd
                    def _(g):
                        for c in range(NCH):
                            g.wait_ge(ss.h, EVW[("s3", c)])
                            if c >= 2:
                                g.wait_ge(sv.h, EVW[("comb", c - 2)])
                            wsrc, wdst, r, c0, cw = chl[c]
                            _q_pool(g, s1W, s2W, s3W, tabWb[c % 2], cw, sq)
                            assert sq.v == EVW[("pool", c)]

                    @blk.tensor
                    def _(tensor):
                        for c in range(NCH):
                            assert stn.v == stnW + c
                            tensor.wait_ge(sv.h, EVW[("post", c)])
                            if c >= 1:
                                tensor.wait_ge(ss.h, EVW[("e", c - 1)])
                            wsrc, wdst, r, c0, cw = chl[c]
                            m = None
                            for k in range(cw // 128):
                                m = tensor.transpose(psT[:, 128 * k:128 * (k + 1)],
                                                     wqW[c % 2][:, 128 * k:128 * (k + 1)], ident[:])
                            stn.inc(m)

                    @blk.scalar
                    def _(scal):
                        for c in range(NCH + 1):
                            if c <= NCH - 1:
                                wsrc, wdst, r, c0, cw = chl[c]
                                scal.wait_ge(sv.h, EVW[("v", c)])
                                if c >= 2:
                                    scal.wait_ge(sv.h, EVW[("L3", c - 2)])
                                _q_scal(scal, vWb[c % 2][:, 0:cw], a2Wb[c % 2][:, 0:cw],
                                        s1W[:, 0:cw], s2W[:, 0:cw], s3W[:, 0:cw], qbias, ss,
                                        s_wait=(sq.h, EVW[("pool", c - 1)] - 2) if c >= 1 else None,
                                        s3_wait=(sq.h, EVW[("pool", c - 1)]) if c >= 1 else None)
                                assert ss.v == EVW[("s3", c)]
                            if c >= 1:
                                b = c - 1
                                wsrc, wdst, r, c0, cw = chl[b]
                                if b >= 1:
                                    scal.wait_ge(sv.h, EVW[("q1", b - 1)])
                                ss.inc(scal.activation(sgnW[:, 0:cw], vWb[b % 2][:, 0:cw], AF.Sign))
                                assert ss.v == EVW[("sgn", b)]
                                scal.wait_ge(stn.h, EVW[("t", b)])
                                wsrc, wdst, r, c0, cw = chl[b]
                                for k in range(cw // 128):
                                    kk = c0 // 128 + k
                                    ss.inc(scal.copy(wdst[:, kk, 128 * r:128 * (r + 1)],
                                                     psT[:, 128 * k:128 * (k + 1)]))
                                assert ss.v == EVW[("e", b)]

            g2a.close()
            pairbuf = g2sb("pairbuf", [128, 2, 2048], F32)
            import os as _os
            if _os.environ.get('KPRINT'):
                print('sbuf free bytes/partition during D:', nc.sbuf_bytes_remaining)

            nic_chunks = [(0, 512), (512, 512), (1024, 512), (1536, 512), (2048, 512), (2560, 256)]
            pair_cols = [(0, 1024), (1024, 1024), (2048, 768)]
            NTOT = NSEQ * NT if stage >= 7 else 0
            xqb = [sexp, attn_tm]
            pp = [sc0, sc1]
            rows = [(s * SEQ + 128 * tt) for s in range(NSEQ) for tt in range(NT)]
            svD, ssD, stnD, ldD, soutD, spD = sv.v, ss.v, stn.v, ld.v, sout.v, sp.v

            def cpf(gt):
                v = stnD
                for g in range(gt):
                    v += (3 if g <= NTOT - 1 else 0) + (4 if g >= 1 else 0)
                return v

            def csf(gt):
                v = ssD
                for g in range(gt):
                    v += (12 if g <= NTOT - 1 else 0) + (3 if g >= 1 else 0)
                return v

            def cvf(gt):
                v = svD
                for g in range(gt):
                    v += (63 if g <= NTOT - 1 else 0) + (2 if g >= 1 else 0)
                return v

            def pairv(gt, pq):
                return cpf(gt) + pq + 1

            def tgroup(gtm1, g):
                gt = gtm1 + 1
                return cpf(gt) + (3 if gt <= NTOT - 1 else 0) + g + 1

            def down_done(gtm1):
                return tgroup(gtm1, 2) + 1

            def spair(gt, pq, k):
                return csf(gt) + 4 * pq + k + 1

            def evacv(gtm1, g):
                gt = gtm1 + 1
                return csf(gt) + (12 if gt <= NTOT - 1 else 0) + g + 1

            def hmultv(gt, ci):
                pq, k = ci // 2, ci % 2
                return cvf(gt) + 21 * pq + k + 1

            def quantdonev(gt):
                return cvf(gt) + 63

            def addsv(gtm1):
                return cvf(gtm1 + 1) + 63 * (1 if gtm1 + 1 <= NTOT - 1 else 0) + 2

            def ldv(gt):
                return ldD + 32 * (gt + 1)

            gbank = [psQ, psB]
            if NTOT:
                with nc.Block() as blk:
                    @blk.sync
                    def _(sync):
                        for gt in range(NTOT):
                            if gt >= 1:
                                sync.wait_ge(ld.h, ldv(gt - 1))
                            if gt >= 2:
                                sync.wait_ge(sp.h, spD + gt - 1)
                                sync.wait_ge(stn.h, pairv(gt - 2, 2))
                            sync.dma_start(
                                out=xqb[gt % 2][:].rearrange("p (c t) -> p c t", t=128),
                                in_=xqT_dram[gt // NT, :, :, 128 * (gt % NT):128 * (gt % NT) + 128].rearrange("c p t -> p c t")).then_inc(ld.h, 16)
                            sync.dma_start(out=pp[gt % 2][:],
                                           in_=partial_dram[rows[gt]:rows[gt] + 128, :]).then_inc(ld.h, 16)
                            ld.v += 32

                    @blk.tensor
                    def _(tensor):
                        for gt in range(NTOT + 1):
                            assert stn.v == cpf(gt), (gt, stn.v, cpf(gt))
                            if gt <= NTOT - 1:
                                tensor.wait_ge(ld.h, ldv(gt))
                                for pq in range(3):
                                    m = None
                                    for sub in range(4):
                                        if pq >= 1:
                                            tensor.wait_ge(ss.h, spair(gt, pq - 1, sub))
                                        elif gt >= 1:
                                            tensor.wait_ge(ss.h, spair(gt - 1, 2, sub))
                                        k = sub % 2
                                        ci = 2 * pq + k
                                        i0, iw = nic_chunks[ci]
                                        if sub < 2:
                                            dst, w_src = gbank[k][:, 0:iw], wgqT
                                        else:
                                            dst = psU[:, 0:iw] if k == 0 else psB[:, 512:512 + iw]
                                            w_src = wuqT
                                        for c in range(NKC):
                                            m = tensor.matmul(dst, xqb[gt % 2][:, 128 * c:128 * (c + 1)],
                                                              w_src[:, c, i0:i0 + iw],
                                                              start=(c == 0), stop=(c == NKC - 1))
                                    stn.inc(m)
                                    assert stn.v == pairv(gt, pq)
                            if gt >= 1:
                                for g in range(3):
                                    tensor.wait_ge(sv.h, cvf(gt - 1) + 21 * (g + 1))
                                    if g == 0 and gt >= 2:
                                        tensor.wait_ge(ss.h, evacv(gt - 2, 2))
                                    if g > 0:
                                        tensor.wait_ge(ss.h, evacv(gt - 1, g - 1))
                                    ncg = min(8, NIC - 8 * g)
                                    m = None
                                    for c in range(ncg):
                                        m = tensor.transpose(
                                            psT[:, 128 * c:128 * (c + 1)],
                                            hq2[:, (gt - 1) % 2, 128 * (8 * g + c):128 * (8 * g + c + 1)],
                                            ident[:])
                                    stn.inc(m)
                                tensor.wait_ge(ss.h, evacv(gt - 1, 2))
                                if gt >= 2:
                                    tensor.wait_ge(sv.h, addsv(gt - 2))
                                m = None
                                for oc in range(2):
                                    for c in range(NIC):
                                        m = tensor.matmul(psA[:, 512 * oc:512 * (oc + 1)],
                                                          hqT[:, 128 * c:128 * (c + 1)],
                                                          wdqT[:, c, 512 * oc:512 * (oc + 1)],
                                                          start=(c == 0), stop=(c == NIC - 1))
                                stn.inc(m)
                                assert stn.v == down_done(gt - 1)

                    @blk.scalar
                    def _(scal):
                        for gt in range(NTOT + 1):
                            assert ss.v == csf(gt), (gt, ss.v, csf(gt))
                            if gt <= NTOT - 1:
                                for pq in range(3):
                                    scal.wait_ge(stn.h, pairv(gt, pq))
                                    if gt * 3 + pq >= 2:
                                        ppq = gt * 3 + pq - 2
                                        scal.wait_ge(sv.h, hmultv(ppq // 3, 2 * (ppq % 3) + 1))
                                    pb2 = pairbuf[:, (gt * 3 + pq) % 2, :]
                                    iwA = nic_chunks[2 * pq][1]
                                    iwB = nic_chunks[2 * pq + 1][1]
                                    gbA = gbank[0]
                                    gbB = gbank[1]
                                    ss.inc(scal.activation(pb2[:, 0:iwA], gbA[:, 0:iwA], AF.Silu))
                                    ss.inc(scal.activation(pb2[:, 512:512 + iwB], gbB[:, 0:iwB], AF.Silu))
                                    ss.inc(scal.copy(pb2[:, 1024:1024 + iwA], psU[:, 0:iwA]))
                                    ss.inc(scal.copy(pb2[:, 1536:1536 + iwB], psB[:, 512:512 + iwB]))
                            if gt >= 1:
                                for g in range(3):
                                    scal.wait_ge(stn.h, tgroup(gt - 1, g))
                                    ncg = min(8, NIC - 8 * g)
                                    ss.inc(scal.copy(hqT[:, 128 * 8 * g:128 * (8 * g + ncg)],
                                                     psT[:, 0:128 * ncg]))

                    @blk.vector
                    def _(vec):
                        for gt in range(NTOT + 1):
                            assert sv.v == cvf(gt), (gt, sv.v, cvf(gt))
                            if gt <= NTOT - 1:
                                if gt >= 2:
                                    vec.wait_ge(stn.h, tgroup(gt - 2, 2))
                                for pq in range(3):
                                    pb2 = pairbuf[:, (gt * 3 + pq) % 2, :]
                                    iwA = nic_chunks[2 * pq][1]
                                    iwB = nic_chunks[2 * pq + 1][1]
                                    vec.wait_ge(ss.h, spair(gt, pq, 2))
                                    sv.inc(vec.tensor_mul(sc2[:, 0:iwA], pb2[:, 0:iwA], pb2[:, 1024:1024 + iwA]))
                                    vec.wait_ge(ss.h, spair(gt, pq, 3))
                                    sv.inc(vec.tensor_mul(sc2[:, iwA:iwA + iwB], pb2[:, 512:512 + iwB],
                                                          pb2[:, 1536:1536 + iwB]))
                                    pw = iwA + iwB
                                    _emit_quant(vec, sc2[:, 0:pw], pw // 16, bstats,
                                                sc3, sc4, sc5, sc6,
                                                hq2[:, gt % 2, pair_cols[pq][0]:pair_cols[pq][0] + pw], sv)
                            if gt >= 1:
                                vec.wait_ge(stn.h, down_done(gt - 1))
                                vec.wait_ge(ld.h, ldv(gt - 1))
                                b = (gt - 1) % 2
                                sv.inc(vec.tensor_add(pp[b][:, 0:512], psA[:, 0:512], pp[b][:, 0:512]))
                                sv.inc(vec.tensor_add(pp[b][:, 512:1024], psA[:, 512:1024], pp[b][:, 512:1024]))
                                assert sv.v == addsv(gt - 1)

                    @blk.gpsimd
                    def _(gpsimd):
                        for gt in range(1, NTOT + 1):
                            gpsimd.wait_ge(sv.h, addsv(gt - 1))
                            gpsimd.dma_start(out=out[rows[gt - 1]:rows[gt - 1] + 128, :],
                                             in_=pp[(gt - 1) % 2][:]).then_inc(sout.h, 16)
                            sout.v += 16
                            gpsimd.wait_ge(sout.h, soutD + 16 * gt)
                            gpsimd.sem_inc(sp.h, 1)
                            sp.v += 1

            with nc.Block() as blk:
                @blk.gpsimd
                def _(gpsimd):
                    gpsimd.wait_ge(sout.h, sout.v)

    return nc


def _ssel():
    s = np.zeros((128, 128), dtype=ml_dtypes.bfloat16)
    s[0, 0:64] = 1.0
    s[64, 64:128] = 1.0
    return s


def kernel(L0_attn_x, L0_attn_Wqkv, L0_attn_bqkv, L0_attn_Wproj, L0_attn_bproj,
           L0_attn_mapping, L0_ffn_w_gate, L0_ffn_w_up, L0_ffn_w_down):
    x = np.ascontiguousarray(np.asarray(L0_attn_x, dtype=np.float32).reshape(16 * SEQ, D))
    consts = dict(
        Wqkv=np.ascontiguousarray(np.asarray(L0_attn_Wqkv, np.float32)),
        bqkv=np.ascontiguousarray(np.asarray(L0_attn_bqkv, np.float32).reshape(1, I3)),
        Wproj=np.ascontiguousarray(np.asarray(L0_attn_Wproj, np.float32)),
        bproj=np.ascontiguousarray(np.asarray(L0_attn_bproj, np.float32).reshape(1, D)),
        mapping=np.ascontiguousarray(np.asarray(L0_attn_mapping, np.int32).reshape(4096, 1)),
        wg=np.ascontiguousarray(np.asarray(L0_ffn_w_gate, np.float32)),
        wu=np.ascontiguousarray(np.asarray(L0_ffn_w_up, np.float32)),
        wd=np.ascontiguousarray(np.asarray(L0_ffn_w_down, np.float32)),
        ident_in=np.eye(128, dtype=ml_dtypes.bfloat16),
        ones_in=np.ones((1, 128), dtype=ml_dtypes.bfloat16),
        etop_in=np.concatenate([np.ones((1, 64)), np.zeros((1, 64))], axis=1).astype(ml_dtypes.bfloat16),
        ebot_in=np.concatenate([np.zeros((1, 64)), np.ones((1, 64))], axis=1).astype(ml_dtypes.bfloat16),
        ssel_in=_ssel(),
    )
    nc = build_nc()
    in_maps = []
    for c in range(8):
        m = dict(consts)
        m["xin"] = np.ascontiguousarray(x[c * NSEQ * SEQ:(c + 1) * NSEQ * SEQ, :])
        in_maps.append(m)
    res = run_bass_kernel_spmd(nc, in_maps, core_ids=list(range(8)))
    outs = [np.asarray(res.results[c]["out"]) for c in range(8)]
    return np.concatenate(outs, axis=0).reshape(16, SEQ, D)


# version 16 (final)
# speedup vs baseline: 2.4373x; 1.0039x over previous
import numpy as np
import ml_dtypes
import concourse.bass as bass
import concourse.mybir as mybir
from concourse.bass_utils import run_bass_kernel_spmd
from concourse.bass_core_rust import IndirectOffsetOnAxis
from contextlib import ExitStack

F32 = mybir.dt.float32
BF16 = mybir.dt.bfloat16
I32 = mybir.dt.int32
AF = mybir.ActivationFunctionType
ALU = mybir.AluOpType
AX = mybir.AxisListType

SEQ = 2048
D = 1024
NT = SEQ // 128
NSEQ = 2
H = 16
I3 = 3072
FI = 2816
NKC = D // 128
NIC = FI // 128
SCALE = 0.125

TL = [0.0625, 0.5625, 1.5625, 3.0625]
TSQ = [6.25, 12.25, 25.0]
QUANT_INCS = 19
QPRE_INCS = 5
QPOST_U = 7
QSC_INCS = 4
QPL_INCS = 3


class Sem:
    def __init__(self, handle):
        self.h = handle
        self.v = 0

    def inc(self, inst, n=1):
        inst.then_inc(self.h, n)
        self.v += n
        return self.v


def _emit_quant(vec, x_ap, nblk, bstats, v, a2, L, S, out_bf, sv):
    n = 16 * nblk
    sc = bstats[:, 0:nblk]
    rsc = bstats[:, 176:176 + nblk]
    hsc = bstats[:, 352:352 + nblk]
    seg = lambda ap: ap.rearrange("p (g s) -> p g s", s=16)
    bc = lambda ap: ap.rearrange("p g -> p g ()").broadcast_to((128, nblk, 16))
    vec.wait_ge(sv.h, sv.v)
    sv.inc(vec.tensor_reduce(sc, seg(x_ap), AX.X, ALU.max, apply_absolute_value=True))
    vec.wait_ge(sv.h, sv.v)
    sv.inc(vec.tensor_scalar(sc, sc, 1.0 / 6.0, 1e-12, ALU.mult, ALU.max))
    vec.wait_ge(sv.h, sv.v)
    sv.inc(vec.reciprocal(rsc, sc))
    sv.inc(vec.tensor_scalar_mul(hsc, sc, 0.5))
    vec.wait_ge(sv.h, sv.v)
    sv.inc(vec.tensor_tensor(seg(v[:, 0:n]), seg(x_ap), bc(rsc), ALU.mult))
    vec.wait_ge(sv.h, sv.v)
    sv.inc(vec.tensor_mul(a2[:, 0:n], v[:, 0:n], v[:, 0:n]))
    vec.wait_ge(sv.h, sv.v)
    sv.inc(vec.tensor_single_scalar(L[:, 0:n], a2[:, 0:n], TL[0], ALU.is_gt))
    for c in TL[1:]:
        vec.wait_ge(sv.h, sv.v)
        sv.inc(vec.scalar_tensor_tensor(L[:, 0:n], a2[:, 0:n], c, L[:, 0:n], ALU.is_gt, ALU.add))
    sv.inc(vec.tensor_single_scalar(S[:, 0:n], a2[:, 0:n], TSQ[2], ALU.is_gt))
    for c in (TSQ[1], TSQ[0], TSQ[2]):
        vec.wait_ge(sv.h, sv.v)
        sv.inc(vec.scalar_tensor_tensor(S[:, 0:n], a2[:, 0:n], c, S[:, 0:n], ALU.is_gt, ALU.add))
    vec.wait_ge(sv.h, sv.v)
    sv.inc(vec.scalar_tensor_tensor(L[:, 0:n], S[:, 0:n], 2.0, L[:, 0:n], ALU.mult, ALU.add))
    sv.inc(vec.tensor_single_scalar(a2[:, 0:n], v[:, 0:n], 0.0, ALU.is_gt))
    vec.wait_ge(sv.h, sv.v)
    sv.inc(vec.tensor_scalar(a2[:, 0:n], a2[:, 0:n], 2.0, -1.0, ALU.mult, ALU.add))
    vec.wait_ge(sv.h, sv.v)
    sv.inc(vec.tensor_mul(L[:, 0:n], L[:, 0:n], a2[:, 0:n]))
    vec.wait_ge(sv.h, sv.v)
    sv.inc(vec.tensor_tensor(seg(out_bf), seg(L[:, 0:n]), bc(hsc), ALU.mult))


def _q_pre(vec, x_ap, nblk, bstats, boff, v, sv):
    n = 16 * nblk
    sc = bstats[:, boff:boff + nblk]
    rsc = bstats[:, 176 + boff:176 + boff + nblk]
    hsc = bstats[:, 352 + boff:352 + boff + nblk]
    seg = lambda ap: ap.rearrange("p (g s) -> p g s", s=16)
    bc = lambda ap: ap.rearrange("p g -> p g ()").broadcast_to((128, nblk, 16))
    vec.wait_ge(sv.h, sv.v)
    sv.inc(vec.tensor_reduce(sc, seg(x_ap), AX.X, ALU.max, apply_absolute_value=True))
    vec.wait_ge(sv.h, sv.v)
    sv.inc(vec.tensor_scalar(sc, sc, 1.0 / 6.0, 1e-12, ALU.mult, ALU.max))
    vec.wait_ge(sv.h, sv.v)
    sv.inc(vec.reciprocal(rsc, sc))
    sv.inc(vec.tensor_scalar_mul(hsc, sc, 0.5))
    vec.wait_ge(sv.h, sv.v)
    sv.inc(vec.tensor_tensor(seg(v[:, 0:n]), seg(x_ap), bc(rsc), ALU.mult))


def _q_scal(scal, v_ap, a2_ap, s1_ap, s2_ap, s3_ap, qbias, ss,
            s_wait=None, s3_wait=None):
    ss.inc(scal.activation(a2_ap, v_ap, AF.Square))
    scal.wait_ge(ss.h, ss.v)
    if s_wait is not None:
        scal.wait_ge(s_wait[0], s_wait[1])
    ss.inc(scal.activation(s1_ap, a2_ap, AF.Sign, bias=qbias[:, 0:1]))
    ss.inc(scal.activation(s2_ap, a2_ap, AF.Sign, bias=qbias[:, 1:2]))
    if s3_wait is not None:
        scal.wait_ge(s3_wait[0], s3_wait[1])
    ss.inc(scal.activation(s3_ap, a2_ap, AF.Sign, bias=qbias[:, 2:3]))


def _q_pool(g, s1, s2, s3, u, n, sq):
    sq.inc(g.tensor_tensor(u[:, 0:n], s1[:, 0:n], s2[:, 0:n], ALU.add))
    g.wait_ge(sq.h, sq.v)
    sq.inc(g.tensor_tensor(u[:, 0:n], u[:, 0:n], s3[:, 0:n], ALU.add))
    g.wait_ge(sq.h, sq.v)
    sq.inc(g.tensor_tensor(u[:, 0:n], u[:, 0:n], s3[:, 0:n], ALU.add))
    g.wait_ge(sq.h, sq.v)


def _q_post(vec, nblk, bstats, boff, a2, sgn, u, L, out_bf, sv, sgn_wait=None,
            s123=None):
    n = 16 * nblk
    hsc = bstats[:, 352 + boff:352 + boff + nblk]
    seg = lambda ap: ap.rearrange("p (g s) -> p g s", s=16)
    bc = lambda ap: ap.rearrange("p g -> p g ()").broadcast_to((128, nblk, 16))
    vec.wait_ge(sv.h, sv.v)
    sv.inc(vec.tensor_scalar(L[:, 0:n], a2[:, 0:n], TL[0], 4.0, ALU.is_gt, ALU.add))
    for c in TL[1:]:
        vec.wait_ge(sv.h, sv.v)
        sv.inc(vec.scalar_tensor_tensor(L[:, 0:n], a2[:, 0:n], c, L[:, 0:n], ALU.is_gt, ALU.add))
    if u is not None:
        vec.wait_ge(sv.h, sv.v)
        sv.inc(vec.scalar_tensor_tensor(L[:, 0:n], u[:, 0:n], 0.0, L[:, 0:n], ALU.add, ALU.add))
    else:
        s1, s2, s3 = s123
        vec.wait_ge(sv.h, sv.v)
        sv.inc(vec.scalar_tensor_tensor(L[:, 0:n], s1[:, 0:n], 0.0, L[:, 0:n], ALU.add, ALU.add))
        vec.wait_ge(sv.h, sv.v)
        sv.inc(vec.scalar_tensor_tensor(L[:, 0:n], s2[:, 0:n], 0.0, L[:, 0:n], ALU.add, ALU.add))
        vec.wait_ge(sv.h, sv.v)
        sv.inc(vec.scalar_tensor_tensor(L[:, 0:n], s3[:, 0:n], 2.0, L[:, 0:n], ALU.mult, ALU.add))
    vec.wait_ge(sv.h, sv.v)
    if sgn_wait is not None:
        vec.wait_ge(sgn_wait[0], sgn_wait[1])
    sv.inc(vec.tensor_mul(L[:, 0:n], L[:, 0:n], sgn[:, 0:n]))
    vec.wait_ge(sv.h, sv.v)
    sv.inc(vec.tensor_tensor(seg(out_bf), seg(L[:, 0:n]), bc(hsc), ALU.mult))


def build_nc(stage=9):
    bsub = stage - 30 if 30 <= stage < 40 else 99
    nc = bass.Bass("TRN2", target_bir_lowering=False)

    xin = nc.dram_tensor("xin", [NSEQ * SEQ, D], F32, kind="ExternalInput").ap()
    Wqkv = nc.dram_tensor("Wqkv", [D, I3], F32, kind="ExternalInput").ap()
    bqkv = nc.dram_tensor("bqkv", [1, I3], F32, kind="ExternalInput").ap()
    Wproj = nc.dram_tensor("Wproj", [D, D], F32, kind="ExternalInput").ap()
    bproj = nc.dram_tensor("bproj", [1, D], F32, kind="ExternalInput").ap()
    mapping = nc.dram_tensor("mapping", [4096, 1], I32, kind="ExternalInput").ap()
    wg = nc.dram_tensor("wg", [FI, D], F32, kind="ExternalInput").ap()
    wu = nc.dram_tensor("wu", [FI, D], F32, kind="ExternalInput").ap()
    wd = nc.dram_tensor("wd", [D, FI], F32, kind="ExternalInput").ap()
    ident_in = nc.dram_tensor("ident_in", [128, 128], BF16, kind="ExternalInput").ap()
    ones_in = nc.dram_tensor("ones_in", [1, 128], BF16, kind="ExternalInput").ap()
    etop_in = nc.dram_tensor("etop_in", [1, 128], BF16, kind="ExternalInput").ap()
    ebot_in = nc.dram_tensor("ebot_in", [1, 128], BF16, kind="ExternalInput").ap()
    ssel_in = nc.dram_tensor("ssel_in", [128, 128], BF16, kind="ExternalInput").ap()

    out = nc.dram_tensor("out", [NSEQ * SEQ, D], F32, kind="ExternalOutput").ap()

    qkv_dram = nc.dram_tensor("qkv_dram", [SEQ, I3], BF16, kind="Internal").ap()
    partial_dram = nc.dram_tensor("partial_dram", [NSEQ * SEQ, D], F32, kind="Internal").ap()
    xqT_dram = nc.dram_tensor("xqT_dram", [NSEQ, NKC, 128, SEQ], BF16, kind="Internal").ap()
    kidx_dram = nc.dram_tensor("kidx_dram", [SEQ, 1], I32, kind="Internal").ap()

    with ExitStack() as st:
        sb = lambda n, s, d: st.enter_context(nc.sbuf_tensor(n, s, d))
        psb = lambda n, s, d: st.enter_context(nc.psum_tensor(n, s, d))
        mksem = lambda n: Sem(st.enter_context(nc.semaphore(n)))

        ld = mksem("ld")
        lg = mksem("lg")
        sv = mksem("sv")
        ss = mksem("ss")
        stn = mksem("stn")
        sout = mksem("sout")
        sp = mksem("sp")
        sq = mksem("sq")

        ident = sb("ident", [128, 128], BF16)
        ones_bf = sb("ones_bf", [1, 128], BF16)
        etop = sb("etop", [1, 128], BF16)
        ebot = sb("ebot", [1, 128], BF16)
        ssel = sb("ssel", [128, 128], BF16)
        onescol = sb("onescol", [128, 1], BF16)

        qidx = sb("qidx", [128, 16], I32)
        kidx = sb("kidx", [128, 16], I32)
        uneg = sb("uneg", [128, 16], F32)
        maskU = sb("maskU", [128, 1024], BF16)

        sc0 = sb("sc0", [128, 1024], F32)
        sc1 = sb("sc1", [128, 1024], F32)
        sc2 = sb("sc2", [128, 1024], F32)
        sc3 = sb("sc3", [128, 1024], F32)
        sc4 = sb("sc4", [128, 1024], F32)
        sc5 = sb("sc5", [128, 1024], F32)
        sc6 = sb("sc6", [128, 1024], F32)
        bstats = sb("bstats", [128, 528], F32)
        qbias = sb("qbias", [128, 3], F32)
        stats = sb("stats", [128, 16], F32)

        sexp = sb("sexp", [128, 1024], BF16)
        attn_tm = sb("attn_tm", [128, 1024], BF16)
        r_sb = sb("r_sb", [128, 16], F32)

        psQ = psb("psQ", [128, 512], F32)
        psU = psb("psU", [128, 512], F32)
        psA = psb("psA", [128, 1024], F32)
        psB = psb("psB", [128, 1024], F32)
        psT = psb("psT", [128, 1024], BF16)
        psS = psb("psS", [128, 32], F32)

        g0 = ExitStack()
        g0sb = lambda n, s, d: g0.enter_context(nc.sbuf_tensor(n, s, d))
        map_col = g0sb("map_col", [128, 16], I32)
        map_colf = g0sb("map_colf", [128, 16], F32)
        u_col = g0sb("u_col", [128, 16], F32)
        u_colbf = g0sb("u_colbf", [128, 16], BF16)
        m64 = g0sb("m64", [32, 1], I32)
        m64f = g0sb("m64f", [32, 4], F32)
        m64i = g0sb("m64i", [32, 1], I32)
        mseg = g0sb("mseg", [32, 256], I32)
        msegf = g0sb("msegf", [32, 256], F32)
        kidx32 = g0sb("kidx32", [32, 64], F32)
        kidx32i = g0sb("kidx32i", [32, 64], I32)
        map_row = g0sb("map_row", [1, 2048], I32)
        map_rowf = g0sb("map_rowf", [1, 2048], F32)
        mask_row = g0sb("mask_row", [1, 2048], BF16)
        mrow_a = g0sb("mrow_a", [1, 512], BF16)
        mrow_b = g0sb("mrow_b", [1, 512], BF16)
        with nc.Block() as blk:
            @blk.sync
            def _(sync):
                sync.dma_start(out=ident[:], in_=ident_in[:, :]).then_inc(ld.h, 16)
                sync.dma_start(out=ones_bf[:], in_=ones_in[:, :]).then_inc(ld.h, 16)
                sync.dma_start(out=etop[:], in_=etop_in[:, :]).then_inc(ld.h, 16)
                sync.dma_start(out=ebot[:], in_=ebot_in[:, :]).then_inc(ld.h, 16)
                sync.dma_start(out=ssel[:], in_=ssel_in[:, :]).then_inc(ld.h, 16)
                sync.dma_start(out=map_row[:],
                               in_=mapping[0:2048, 0].rearrange("(o t) -> o t", o=1)).then_inc(ld.h, 16)
                sync.dma_start(out=map_col[:],
                               in_=mapping[0:2048, 0].rearrange("(j p) -> p j", p=128),
                               allow_slow_non_contiguous=True).then_inc(ld.h, 16)
                sync.dma_start(out=m64[:],
                               in_=mapping[:, 0].rearrange("(n s) -> n s", s=64)[0:32, 0:1],
                               allow_slow_non_contiguous=True).then_inc(ld.h, 16)
                ld.v += 16 * 8

            @blk.vector
            def _(vec):
                vec.wait_ge(ld.h, ld.v)
                sv.inc(vec.memset(onescol[:], 1.0))
                sv.inc(vec.memset(qbias[:, 0:1], -TSQ[0]))
                sv.inc(vec.memset(qbias[:, 1:2], -TSQ[1]))
                sv.inc(vec.memset(qbias[:, 2:3], -TSQ[2]))
                sv.inc(vec.tensor_copy(map_colf[:], map_col[:]))
                vec.wait_ge(sv.h, sv.v)
                sv.inc(vec.tensor_scalar_min(map_colf[:], map_colf[:], 2047.0))
                vec.wait_ge(sv.h, sv.v)
                sv.inc(vec.tensor_copy(qidx[:], map_colf[:]))
                sv.inc(vec.tensor_copy(u_col[:], map_col[:]))
                vec.wait_ge(sv.h, sv.v)
                sv.inc(vec.tensor_single_scalar(u_col[:], u_col[:], 2048.0, ALU.is_lt))
                vec.wait_ge(sv.h, sv.v)
                sv.inc(vec.tensor_copy(u_colbf[:], u_col[:]))
                sv.inc(vec.tensor_copy(m64f[:, 0:1], m64[:]))
                vec.wait_ge(sv.h, sv.v)
                sv.inc(vec.tensor_single_scalar(m64f[:, 0:1], m64f[:, 0:1], 1.0 / 256.0, ALU.mult))
                vec.wait_ge(sv.h, sv.v)
                sv.inc(vec.tensor_copy(m64i[:], m64f[:, 0:1]))
                vec.wait_ge(sv.h, sv.v)
                sv.inc(vec.tensor_copy(m64f[:, 1:2], m64i[:]))
                vec.wait_ge(sv.h, sv.v)
                sv.inc(vec.tensor_tensor(m64f[:, 2:3], m64f[:, 1:2], m64f[:, 0:1], ALU.is_gt))
                vec.wait_ge(sv.h, sv.v)
                sv.inc(vec.tensor_sub(m64f[:, 3:4], m64f[:, 1:2], m64f[:, 2:3]))
                vec.wait_ge(sv.h, sv.v)
                sv.inc(vec.tensor_single_scalar(m64f[:, 3:4], m64f[:, 3:4], 256.0, ALU.mult))
                vec.wait_ge(sv.h, sv.v)
                sv.inc(vec.tensor_copy(m64i[:], m64f[:, 3:4]))
                sv.inc(vec.tensor_copy(map_rowf[:], map_row[:]))
                vec.wait_ge(sv.h, sv.v)
                sv.inc(vec.tensor_single_scalar(map_rowf[:], map_rowf[:], 2048.0, ALU.is_lt))
                vec.wait_ge(sv.h, sv.v)
                sv.inc(vec.tensor_scalar(map_rowf[:], map_rowf[:], -1.0, 8e9, ALU.add, ALU.mult))
                vec.wait_ge(sv.h, sv.v)
                sv.inc(vec.tensor_copy(mask_row[:], map_rowf[:]))

        with nc.Block() as blk:
            @blk.gpsimd
            def _(gpsimd):
                gpsimd.indirect_dma_start(
                    mseg[:], None, mapping[:, :],
                    IndirectOffsetOnAxis(m64i[:], 0)).then_inc(lg.h, 16)
                lg.v += 16
                gpsimd.wait_ge(lg.h, lg.v)
                gpsimd.sem_inc(sp.h, 1)
                sp.v += 1

            @blk.vector
            def _(vec):
                vec.wait_ge(sp.h, sp.v)
                sv.inc(vec.tensor_copy(msegf[:], mseg[:]))
                vec.wait_ge(sv.h, sv.v)
                sv.inc(vec.tensor_scalar_min(kidx32[:], msegf[:, 0:256:4], 2047.0))
                vec.wait_ge(sv.h, sv.v)
                sv.inc(vec.tensor_copy(kidx32i[:], kidx32[:]))

        with nc.Block() as blk:
            @blk.gpsimd
            def _(gpsimd):
                gpsimd.dma_start(out=kidx_dram[:, 0].rearrange("(p f) -> p f", p=32),
                                 in_=kidx32i[:]).then_inc(lg.h, 16)
                lg.v += 16
                gpsimd.wait_ge(lg.h, lg.v)
                gpsimd.dma_start(out=kidx[:],
                                 in_=kidx_dram[:, 0].rearrange("(j p) -> p j", p=128),
                                 allow_slow_non_contiguous=True).then_inc(lg.h, 16)
                lg.v += 16
                gpsimd.wait_ge(lg.h, lg.v)

            @blk.vector
            def _(vec):
                src = mask_row[0:1, :].rearrange("o (j q) -> o j q", q=128)
                sv.inc(vec.tensor_copy(mrow_a[0:1, :].rearrange("o (j q) -> o j q", q=64),
                                       src[:, 0:8, 0:64]))
                sv.inc(vec.tensor_copy(mrow_b[0:1, :].rearrange("o (j q) -> o j q", q=64),
                                       src[:, 0:8, 64:128]))

        with nc.Block() as blk:
            @blk.tensor
            def _(tensor):
                tensor.matmul(psA[:, 0:512], etop[0:1, :], mrow_a[0:1, :], start=True, stop=False)
                stn.inc(tensor.matmul(psA[:, 0:512], ebot[0:1, :], mrow_b[0:1, :], start=False, stop=True))
                stn.inc(tensor.matmul(psS[:, 0:16], ssel[:], u_colbf[:], start=True, stop=True))

            @blk.vector
            def _(vec):
                vec.wait_ge(stn.h, stn.v)
                sv.inc(vec.tensor_copy(maskU[:, 0:512], psA[:, 0:512]))
                sv.inc(vec.tensor_scalar(uneg[:], psS[:, 0:16], -1.0, 1e10, ALU.add, ALU.mult))
                src = mask_row[0:1, :].rearrange("o (j q) -> o j q", q=128)
                sv.inc(vec.tensor_copy(mrow_a[0:1, :].rearrange("o (j q) -> o j q", q=64),
                                       src[:, 8:16, 0:64]))
                sv.inc(vec.tensor_copy(mrow_b[0:1, :].rearrange("o (j q) -> o j q", q=64),
                                       src[:, 8:16, 64:128]))

        with nc.Block() as blk:
            @blk.tensor
            def _(tensor):
                tensor.matmul(psA[:, 512:1024], etop[0:1, :], mrow_a[0:1, :], start=True, stop=False)
                stn.inc(tensor.matmul(psA[:, 512:1024], ebot[0:1, :], mrow_b[0:1, :], start=False, stop=True))

            @blk.vector
            def _(vec):
                vec.wait_ge(stn.h, stn.v)
                sv.inc(vec.tensor_copy(maskU[:, 512:1024], psA[:, 512:1024]))

        g0.close()

        with ExitStack() as g1:
            g1sb = lambda n, s, d: g1.enter_context(nc.sbuf_tensor(n, s, d))
            xnT = g1sb("xnT", [128, 1024], BF16)
            xq_tm = g1sb("xq_tm", [128, 1024], BF16)
            Wqkv_sb = g1sb("Wqkv_sb", [128, NKC, I3], BF16)
            Wproj_sb = g1sb("Wproj_sb", [128, NKC, D], BF16)
            attn_fm = g1sb("attn_fm", [128, NKC, SEQ], BF16)
            Qsel = g1sb("Qsel", [128, 2, 1024], BF16)
            Ksel = g1sb("Ksel", [128, 2, 1024], BF16)
            Vsel = g1sb("Vsel", [128, 2, 1024], BF16)
            sexp2 = g1sb("sexp2", [128, 1024], BF16)
            attn_tm2 = g1sb("attn_tm2", [128, 1024], BF16)
            QselT = g1sb("QselT", [128, 1024], BF16)
            KselT = g1sb("KselT", [128, 1024], BF16)
            qkv_tile = g1sb("qkv_tile", [128, I3], BF16)
            qkv_tile2 = g1sb("qkv_tile2", [128, I3], BF16)
            xnT2 = g1sb("xnT2", [128, 1024], BF16)
            xq_tm2 = g1sb("xq_tm2", [128, 1024], BF16)
            xqTst = g1sb("xqTst", [128, 1024], BF16)
            xqTst2 = g1sb("xqTst2", [128, 1024], BF16)
            bqkv_bf = g1sb("bqkv_bf", [1, I3], BF16)
            bproj_bf = g1sb("bproj_bf", [1, D], BF16)

            with nc.Block() as blk:
                @blk.gpsimd
                def _(gpsimd):
                    gpsimd.dma_start(out=bqkv_bf[:], in_=bqkv[:, :]).then_inc(lg.h, 16)
                    gpsimd.dma_start(out=bproj_bf[:], in_=bproj[:, :]).then_inc(lg.h, 16)
                    for c in range(NKC):
                        gpsimd.dma_start(out=Wqkv_sb[:, c, :],
                                         in_=Wqkv[128 * c:128 * (c + 1), :]).then_inc(lg.h, 16)
                        gpsimd.dma_start(out=Wproj_sb[:, c, :],
                                         in_=Wproj[128 * c:128 * (c + 1), :]).then_inc(lg.h, 16)
                    lg.v += 16 * (2 + 2 * NKC)
                    gpsimd.wait_ge(lg.h, lg.v)

            for seq in range(NSEQ if stage >= 5 else (1 if (stage >= 2 or stage >= 30) else 0)):
                if True:
                    T = NT
                    xbA = [sc0, sc1]
                    xnbfA = [sexp, attn_tm]
                    xqA = [xq_tm, xq_tm2]
                    xnTA = [xnT, xnT2]
                    xqTA = [xqTst, xqTst2]
                    qkvA = [qkv_tile, qkv_tile2]
                    svA, ssA, stnA, ldA, lgA, spA = sv.v, ss.v, stn.v, ld.v, lg.v, sp.v

                    def cva(t):
                        return svA + 26 * t

                    def csa(t):
                        return ssA + (9 * t - 8 if t >= 1 else 0)

                    def cpa(t):
                        return stnA + (8 * t - 6 if t >= 1 else 0)

                    def sc_off(t):
                        return 1 if t <= T - 1 else 0

                    rstdv = lambda t: csa(t) + 1
                    xnTev = lambda tm1: csa(tm1 + 1) + sc_off(tm1 + 1) + 1
                    qkvev = lambda tm1, oc: csa(tm1 + 1) + sc_off(tm1 + 1) + 2 + oc
                    xqTev = lambda tm1: csa(tm1 + 1) + sc_off(tm1 + 1) + 8
                    qkvchunk = lambda tm1, oc: cpa(tm1 + 1) + oc + 1
                    txnv = lambda t: cpa(t) + (6 if t >= 1 else 0) + 1
                    txqv = lambda t: txnv(t) + 1
                    ldav = lambda t: ldA + 16 * (t + 1)

                    with nc.Block() as blk:
                        @blk.sync
                        def _(sync):
                            for t in range(T):
                                if t >= 1:
                                    sync.wait_ge(ld.h, ldav(t - 1))
                                if t >= 2:
                                    sync.wait_ge(sv.h, cva(t - 2) + 6)
                                sync.dma_start(out=xbA[t % 2][:],
                                               in_=xin[seq * SEQ + 128 * t:seq * SEQ + 128 * t + 128, :]).then_inc(ld.h, 16)
                                ld.v += 16

                        @blk.vector
                        def _(vec):
                            for t in range(T):
                                assert sv.v == cva(t), (t, sv.v)
                                vec.wait_ge(ld.h, ldav(t))
                                sv.inc(vec.bn_stats(stats[:, 0:6], xbA[t % 2][:, 0:512]))
                                sv.inc(vec.bn_stats(stats[:, 6:12], xbA[t % 2][:, 512:1024]))
                                vec.wait_ge(sv.h, sv.v)
                                sv.inc(vec.bn_aggr(stats[:, 12:14], stats[:, 0:12]))
                                vec.wait_ge(sv.h, sv.v)
                                sv.inc(vec.tensor_scalar_add(stats[:, 14:15], stats[:, 13:14], 1e-5))
                                vec.wait_ge(sv.h, sv.v)
                                sv.inc(vec.reciprocal(stats[:, 15:16], stats[:, 14:15]))
                                vec.wait_ge(ss.h, rstdv(t))
                                sv.inc(vec.tensor_scalar(sc2[:], xbA[t % 2][:], stats[:, 12:13], stats[:, 14:15],
                                                         ALU.subtract, ALU.mult))
                                if t >= 2:
                                    vec.wait_ge(stn.h, txnv(t - 2))
                                vec.wait_ge(sv.h, sv.v)
                                sv.inc(vec.tensor_copy(xnbfA[t % 2][:], sc2[:]))
                                if t >= 2:
                                    vec.wait_ge(stn.h, txqv(t - 2))
                                _emit_quant(vec, sc2[:], 64, bstats, sc3, sc4, sc5, sc6,
                                            xqA[t % 2][:], sv)

                        @blk.scalar
                        def _(scal):
                            for t in range(T + 1):
                                assert ss.v == csa(t), (t, ss.v)
                                if t <= T - 1:
                                    scal.wait_ge(sv.h, cva(t) + 5)
                                    ss.inc(scal.activation(stats[:, 14:15], stats[:, 15:16], AF.Sqrt))
                                if t >= 1:
                                    scal.wait_ge(stn.h, txnv(t - 1))
                                    ss.inc(scal.copy(xnTA[(t - 1) % 2][:], psT[:]))
                                    for oc in range(6):
                                        scal.wait_ge(stn.h, qkvchunk(t - 1, oc))
                                        if t >= 2 and oc == 0:
                                            scal.wait_ge(sp.h, spA + t - 1)
                                        pq = psQ if oc % 2 == 0 else psU
                                        ss.inc(scal.copy(qkvA[(t - 1) % 2][:, 512 * oc:512 * (oc + 1)],
                                                         pq[:]))
                                    scal.wait_ge(stn.h, txqv(t - 1))
                                    ss.inc(scal.copy(xqTA[(t - 1) % 2][:], psT[:]))

                        @blk.tensor
                        def _(tensor):
                            for t in range(T + 1):
                                assert stn.v == cpa(t), (t, stn.v)
                                if t >= 1:
                                    tensor.wait_ge(ss.h, xnTev(t - 1))
                                    for oc in range(6):
                                        if oc >= 2:
                                            tensor.wait_ge(ss.h, qkvev(t - 1, oc - 2))
                                        pq = psQ if oc % 2 == 0 else psU
                                        for c in range(NKC):
                                            tensor.matmul(pq[:], xnTA[(t - 1) % 2][:, 128 * c:128 * (c + 1)],
                                                          Wqkv_sb[:, c, 512 * oc:512 * (oc + 1)],
                                                          start=(c == 0), stop=False)
                                        stn.inc(tensor.matmul(pq[:], ones_bf[0:1, :],
                                                              bqkv_bf[0:1, 512 * oc:512 * (oc + 1)],
                                                              start=False, stop=True))
                                if t <= T - 1:
                                    vwait = cva(t) + 7
                                    tensor.wait_ge(sv.h, vwait)
                                    if t >= 1:
                                        tensor.wait_ge(ss.h, xqTev(t - 1))
                                    m = None
                                    for c in range(NKC):
                                        m = tensor.transpose(psT[:, 128 * c:128 * (c + 1)],
                                                             xnbfA[t % 2][:, 128 * c:128 * (c + 1)], ident[:])
                                    stn.inc(m)
                                    tensor.wait_ge(sv.h, cva(t + 1))
                                    tensor.wait_ge(ss.h, xnTev(t))
                                    m = None
                                    for c in range(NKC):
                                        m = tensor.transpose(psT[:, 128 * c:128 * (c + 1)],
                                                             xqA[t % 2][:, 128 * c:128 * (c + 1)], ident[:])
                                    stn.inc(m)

                        @blk.gpsimd
                        def _(gpsimd):
                            for t in range(1, T + 1):
                                gpsimd.wait_ge(ss.h, xqTev(t - 1))
                                gpsimd.dma_start(out=qkv_dram[128 * (t - 1):128 * t, :],
                                                 in_=qkvA[(t - 1) % 2][:]).then_inc(lg.h, 16)
                                gpsimd.dma_start(
                                    out=xqT_dram[seq, :, :, 128 * (t - 1):128 * t].rearrange("c p t -> p c t"),
                                    in_=xqTA[(t - 1) % 2][:].rearrange("p (c t) -> p c t", t=128)).then_inc(lg.h, 16)
                                lg.v += 32
                                gpsimd.wait_ge(lg.h, lg.v)
                                gpsimd.sem_inc(sp.h, 1)
                                sp.v += 1

                if stage >= 3 or stage >= 30:
                    T = NT
                    sxB = [sexp, sexp2]
                    atB = [attn_tm, attn_tm2]
                    svB, ssB, stnB, lgB, spB = sv.v, ss.v, stn.v, lg.v, sp.v

                    EV = {}

                    def prepass():
                        cs, cv, cp, csp = ssB, svB, stnB, spB
                        for j in range(T + 2):
                            if j <= T - 1:
                                csp += 1
                                EV[("gath", j)] = csp
                            if j >= 2:
                                cp += 1; EV[("attnT", j - 2)] = cp
                            if j <= T - 1:
                                cp += 1; EV[("tq", j)] = cp
                                cp += 1; EV[("tk", j)] = cp
                                cp += 2; EV[("scores", j)] = cp
                            if 1 <= j <= T:
                                cp += 2; EV[("pv", j - 1)] = cp
                            if j >= 2:
                                cs += 1; EV[("fmcopy", j - 2)] = cs
                            if j <= T - 1:
                                cs += 1; EV[("evq", j)] = cs
                                cs += 1; EV[("evk", j)] = cs
                                cs += 1; EV[("exp", j)] = cs
                            if j <= T - 1:
                                cv += 1; EV[("masked", j)] = cv
                                cv += 1; EV[("colmax", j)] = cv
                                cv += 1; EV[("subm", j)] = cv
                            if 1 <= j <= T:
                                cv += 1; EV[("radd", j - 1)] = cv
                                cv += 1; EV[("recip", j - 1)] = cv
                                cv += 2; EV[("evacTT", j - 1)] = cv
                        EV["end_cp"] = cp
                        EV["end_cs"] = cs
                        EV["end_cv"] = cv
                        EV["end_csp"] = csp
                    prepass()

                    with nc.Block() as blk:
                        @blk.gpsimd
                        def _(gpsimd):
                            for j in range(T):
                                if j >= 2:
                                    gpsimd.wait_ge(stn.h, EV[("pv", j - 2)])
                                b = j % 2
                                gpsimd.indirect_dma_start(
                                    Qsel[:, b, :], None, qkv_dram[:, :],
                                    IndirectOffsetOnAxis(qidx[:, j:j + 1], 0)).then_inc(lg.h, 16)
                                gpsimd.indirect_dma_start(
                                    Ksel[:, b, :], None, qkv_dram[:, :],
                                    IndirectOffsetOnAxis(kidx[:, j:j + 1], 0),
                                    element_offset=1024).then_inc(lg.h, 16)
                                gpsimd.indirect_dma_start(
                                    Vsel[:, b, :], None, qkv_dram[:, :],
                                    IndirectOffsetOnAxis(kidx[:, j:j + 1], 0),
                                    element_offset=2048).then_inc(lg.h, 16)
                                lg.v += 48
                                gpsimd.wait_ge(lg.h, lg.v)
                                gpsimd.sem_inc(sp.h, 1)
                                sp.v += 1
                                assert sp.v == EV[("gath", j)], (j, sp.v)

                        @blk.tensor
                        def _(tensor):
                            for j in range(T + 2):
                                if j >= 2:
                                    jj = j - 2
                                    tensor.wait_ge(sv.h, EV[("evacTT", jj)])
                                    if jj >= 1:
                                        tensor.wait_ge(ss.h, EV[("fmcopy", jj - 1)])
                                    m = None
                                    for c in range(NKC):
                                        m = tensor.transpose(psT[:, 128 * c:128 * (c + 1)],
                                                             atB[jj % 2][:, 128 * c:128 * (c + 1)], ident[:])
                                    stn.inc(m)
                                    assert stn.v == EV[("attnT", jj)]
                                if j <= T - 1:
                                    b = j % 2
                                    tensor.wait_ge(sp.h, EV[("gath", j)])
                                    if j >= 2:
                                        tensor.wait_ge(ss.h, EV[("fmcopy", j - 2)])
                                    m = None
                                    for c in range(NKC):
                                        m = tensor.transpose(psT[:, 128 * c:128 * (c + 1)],
                                                             Qsel[:, b, 128 * c:128 * (c + 1)], ident[:])
                                    stn.inc(m)
                                    assert stn.v == EV[("tq", j)]
                                    tensor.wait_ge(ss.h, EV[("evq", j)])
                                    m = None
                                    for c in range(NKC):
                                        m = tensor.transpose(psT[:, 128 * c:128 * (c + 1)],
                                                             Ksel[:, b, 128 * c:128 * (c + 1)], ident[:])
                                    stn.inc(m)
                                    assert stn.v == EV[("tk", j)]
                                    tensor.wait_ge(ss.h, EV[("evk", j)])
                                    if j >= 2:
                                        tensor.wait_ge(sv.h, EV[("subm", j - 2)])
                                    psAB = psA if b == 0 else psB
                                    m = None
                                    for parity in range(2):
                                        if parity == 1:
                                            stn.inc(m)
                                            tensor.wait_ge(stn.h, stn.v)
                                        pb = 64 * parity
                                        for h in range(parity, H, 2):
                                            cc = 128 * (h // 2)
                                            for half in range(2):
                                                m = tensor.matmul(
                                                    psAB[64 * half:64 * half + 64, 64 * h:64 * h + 64],
                                                    KselT[pb:pb + 64, cc + 64 * half:cc + 64 * half + 64],
                                                    QselT[pb:pb + 64, cc + 64 * half:cc + 64 * half + 64],
                                                    start=True, stop=True)
                                    stn.inc(m)
                                    assert stn.v == EV[("scores", j)]
                                if 1 <= j <= T:
                                    jj = j - 1
                                    tensor.wait_ge(ss.h, EV[("exp", jj)])
                                    if jj >= 2:
                                        tensor.wait_ge(sv.h, EV[("evacTT", jj - 2)])
                                        tensor.wait_ge(sv.h, EV[("radd", jj - 2)])
                                    m = None
                                    sx = sxB[jj % 2]
                                    for half in range(2):
                                        if half == 1:
                                            stn.inc(m)
                                            tensor.wait_ge(stn.h, stn.v)
                                        pb = 64 * half
                                        for h in range(H):
                                            po = psQ if h < 8 else psU
                                            co = 64 * h if h < 8 else 64 * (h - 8)
                                            tensor.matmul(po[pb:pb + 64, co:co + 64],
                                                          sx[pb:pb + 64, 64 * h:64 * h + 64],
                                                          Vsel[pb:pb + 64, jj % 2, 64 * h:64 * h + 64],
                                                          start=True, stop=True)
                                            m = tensor.matmul(psS[pb:pb + 64, 16 * (jj % 2) + h:16 * (jj % 2) + h + 1],
                                                              sx[pb:pb + 64, 64 * h:64 * h + 64],
                                                              onescol[pb:pb + 64, :],
                                                              start=True, stop=True)
                                    stn.inc(m)
                                    assert stn.v == EV[("pv", jj)]
                            assert stn.v == EV["end_cp"]

                        @blk.scalar
                        def _(scal):
                            for j in range(T + 2):
                                if j >= 2:
                                    jj = j - 2
                                    scal.wait_ge(stn.h, EV[("attnT", jj)])
                                    ss.inc(scal.copy(attn_fm[:, :, 128 * jj:128 * (jj + 1)],
                                                     psT[:].rearrange("p (c t) -> p c t", t=128)))
                                    assert ss.v == EV[("fmcopy", jj)]
                                if j <= T - 1:
                                    scal.wait_ge(stn.h, EV[("tq", j)])
                                    ss.inc(scal.copy(QselT[:], psT[:]))
                                    scal.wait_ge(stn.h, EV[("tk", j)])
                                    ss.inc(scal.copy(KselT[:], psT[:]))
                                    scal.wait_ge(sv.h, EV[("subm", j)])
                                    ss.inc(scal.activation(sxB[j % 2][:], sc4[:], AF.Exp, scale=SCALE))
                                    assert ss.v == EV[("exp", j)]
                            assert ss.v == EV["end_cs"]

                        @blk.vector
                        def _(vec):
                            for j in range(T + 2):
                                if j <= T - 1:
                                    b = j % 2
                                    psAB = psA if b == 0 else psB
                                    vec.wait_ge(stn.h, EV[("scores", j)])
                                    if j >= 1:
                                        vec.wait_ge(ss.h, EV[("exp", j - 1)])
                                    sv.inc(vec.tensor_tensor(
                                        sc3[:].rearrange("p (h q) -> p h q", q=64),
                                        psAB[:].rearrange("p (h q) -> p h q", q=64),
                                        maskU[:, 64 * j:64 * j + 64].rearrange("p q -> p () q").broadcast_to((128, 16, 64)),
                                        ALU.add))
                                    vec.wait_ge(sv.h, sv.v)
                                    sv.inc(vec.tensor_reduce(stats[:, 0:16],
                                                             sc3[:].rearrange("p (h q) -> p h q", q=64),
                                                             AX.X, ALU.max))
                                    vec.wait_ge(sv.h, sv.v)
                                    sv.inc(vec.scalar_tensor_tensor(
                                        sc4[:].rearrange("p (h q) -> p h q", q=64),
                                        sc3[:].rearrange("p (h q) -> p h q", q=64),
                                        uneg[:, j:j + 1],
                                        stats[:, 0:16].rearrange("p h -> p h ()").broadcast_to((128, 16, 64)),
                                        ALU.add, ALU.subtract))
                                    assert sv.v == EV[("subm", j)]
                                if 1 <= j <= T:
                                    jj = j - 1
                                    vec.wait_ge(stn.h, EV[("pv", jj)])
                                    sv.inc(vec.tensor_scalar_add(r_sb[:], psS[:, 16 * (jj % 2):16 * (jj % 2) + 16], 1e-6))
                                    vec.wait_ge(sv.h, sv.v)
                                    sv.inc(vec.reciprocal(r_sb[:], r_sb[:]))
                                    vec.wait_ge(sv.h, sv.v)
                                    at = atB[jj % 2]
                                    if jj >= 2:
                                        vec.wait_ge(stn.h, EV[("attnT", jj - 2)])
                                    sv.inc(vec.tensor_tensor(
                                        at[:, 0:512].rearrange("p (g d) -> p g d", d=64),
                                        psQ[:].rearrange("p (g d) -> p g d", d=64),
                                        r_sb[:, 0:8].rearrange("p h -> p h ()").broadcast_to((128, 8, 64)),
                                        ALU.mult))
                                    sv.inc(vec.tensor_tensor(
                                        at[:, 512:1024].rearrange("p (g d) -> p g d", d=64),
                                        psU[:].rearrange("p (g d) -> p g d", d=64),
                                        r_sb[:, 8:16].rearrange("p h -> p h ()").broadcast_to((128, 8, 64)),
                                        ALU.mult))
                                    assert sv.v == EV[("evacTT", jj)]
                            assert sv.v == EV["end_cv"]

                if stage >= 4:
                    T = NT
                    cbC = [sc0, sc1]
                    pbC = [sc2, sc3]
                    svC, stnC, ldC, lgC, spC = sv.v, stn.v, ld.v, lg.v, sp.v
                    projv = lambda t: stnC + 2 * (t + 1)
                    addsv_c = lambda t: svC + 2 * (t + 1)
                    ldcv = lambda t: ldC + 16 * (t + 1)

                    with nc.Block() as blk:
                        @blk.sync
                        def _(sync):
                            for t in range(T):
                                if t >= 1:
                                    sync.wait_ge(ld.h, ldcv(t - 1))
                                if t >= 2:
                                    sync.wait_ge(sv.h, addsv_c(t - 2))
                                sync.dma_start(out=cbC[t % 2][:],
                                               in_=xin[seq * SEQ + 128 * t:seq * SEQ + 128 * t + 128, :]).then_inc(ld.h, 16)
                                ld.v += 16

                        @blk.tensor
                        def _(tensor):
                            for t in range(T):
                                assert stn.v == stnC + 2 * t
                                if t >= 2:
                                    tensor.wait_ge(sv.h, addsv_c(t - 2))
                                banks = (psQ[:], psU[:]) if t % 2 == 0 else (psA[:, 0:512], psA[:, 512:1024])
                                for oc in range(2):
                                    for c in range(NKC):
                                        tensor.matmul(banks[oc], attn_fm[:, c, 128 * t:128 * (t + 1)],
                                                      Wproj_sb[:, c, 512 * oc:512 * (oc + 1)],
                                                      start=(c == 0), stop=False)
                                    stn.inc(tensor.matmul(banks[oc], ones_bf[0:1, :],
                                                          bproj_bf[0:1, 512 * oc:512 * (oc + 1)],
                                                          start=False, stop=True))

                        @blk.vector
                        def _(vec):
                            for t in range(T):
                                assert sv.v == svC + 2 * t
                                vec.wait_ge(stn.h, projv(t))
                                vec.wait_ge(ld.h, ldcv(t))
                                if t >= 2:
                                    vec.wait_ge(sp.h, spC + t - 1)
                                banks = (psQ[:], psU[:]) if t % 2 == 0 else (psA[:, 0:512], psA[:, 512:1024])
                                sv.inc(vec.tensor_add(pbC[t % 2][:, 0:512], banks[0], cbC[t % 2][:, 0:512]))
                                sv.inc(vec.tensor_add(pbC[t % 2][:, 512:1024], banks[1], cbC[t % 2][:, 512:1024]))

                        @blk.gpsimd
                        def _(gpsimd):
                            for t in range(1, T + 1):
                                gpsimd.wait_ge(sv.h, addsv_c(t - 1))
                                gpsimd.dma_start(
                                    out=partial_dram[seq * SEQ + 128 * (t - 1):seq * SEQ + 128 * t, :],
                                    in_=pbC[(t - 1) % 2][:]).then_inc(lg.h, 16)
                                lg.v += 16
                                gpsimd.wait_ge(lg.h, lg.v)
                                gpsimd.sem_inc(sp.h, 1)
                                sp.v += 1

        with ExitStack() as g2:
            g2sb = lambda n, s, d: g2.enter_context(nc.sbuf_tensor(n, s, d))
            wgqT = g2sb("wgqT", [128, NKC, FI], BF16)
            wuqT = g2sb("wuqT", [128, NKC, FI], BF16)
            wdqT = g2sb("wdqT", [128, NIC, D], BF16)
            hq2 = g2sb("hq2", [128, 2, FI], BF16)
            hqT = g2sb("hqT", [128, FI], BF16)
            g2a = ExitStack()
            wrow = g2a.enter_context(nc.sbuf_tensor("wrow", [128, 1024], F32))
            wq_bf = g2a.enter_context(nc.sbuf_tensor("wq_bf", [128, 1024], BF16))
            sgnW = g2a.enter_context(nc.sbuf_tensor("sgnW", [128, 1024], BF16))
            s1W = g2a.enter_context(nc.sbuf_tensor("s1W", [128, 1024], BF16))
            s2W = g2a.enter_context(nc.sbuf_tensor("s2W", [128, 1024], BF16))
            s3W = g2a.enter_context(nc.sbuf_tensor("s3W", [128, 1024], BF16))

            wrow2 = g2a.enter_context(nc.sbuf_tensor("wrow2", [128, 1024], F32))
            wq_bf2 = g2a.enter_context(nc.sbuf_tensor("wq_bf2", [128, 1024], BF16))

            if stage >= 6:
                chl = []
                for wsrc, wdst, nrch, ncols in [(wg, wgqT, NIC, D), (wu, wuqT, NIC, D), (wd, wdqT, NKC, FI)]:
                    c0 = 0
                    cols = []
                    while c0 < ncols:
                        cols.append((c0, min(1024, ncols - c0)))
                        c0 += cols[-1][1]
                    for r in range(nrch):
                        for (c0, cw) in cols:
                            chl.append((wsrc, wdst, r, c0, cw))
                NCH = len(chl)
                wrW = [wrow, wrow2]
                wqW = [wq_bf, wq_bf2]
                vWb = [sc3, sc0]
                a2Wb = [sc4, sc6]
                tabWb = [sc1, sc2]
                svW, ssW, stnW, ldW, sqW = sv.v, ss.v, stn.v, ld.v, sq.v
                EVW = {}
                cs, cv, cp, cq = ssW, svW, stnW, sqW
                for c in range(NCH + 1):
                    if c <= NCH - 1:
                        cv += QPRE_INCS
                        EVW[("v", c)] = cv
                    if c >= 1:
                        cv += QPOST_U
                        EVW[("post", c - 1)] = cv
                        EVW[("L3", c - 1)] = cv - 3
                        EVW[("comb", c - 1)] = cv - 2
                        EVW[("q1", c - 1)] = cv - 1
                    if c <= NCH - 1:
                        cs += 1
                        EVW[("a2", c)] = cs
                        cs += 3
                        EVW[("s3", c)] = cs
                    if c >= 1:
                        cs += 1
                        EVW[("sgn", c - 1)] = cs
                        cs += chl[c - 1][4] // 128
                        EVW[("e", c - 1)] = cs
                    if c <= NCH - 1:
                        cq += QPL_INCS
                        EVW[("pool", c)] = cq
                    if c <= NCH - 1:
                        cp += 1
                        EVW[("t", c)] = cp
                ldwv = lambda c: ldW + 16 * (c + 1)

                with nc.Block() as blk:
                    @blk.sync
                    def _(sync):
                        for c in range(NCH):
                            if c >= 1:
                                sync.wait_ge(ld.h, ldwv(c - 1))
                            if c >= 2:
                                sync.wait_ge(sv.h, EVW[("v", c - 2)])
                            wsrc, wdst, r, c0, cw = chl[c]
                            sync.dma_start(out=wrW[c % 2][:, 0:cw],
                                           in_=wsrc[128 * r:128 * (r + 1), c0:c0 + cw]).then_inc(ld.h, 16)
                            ld.v += 16

                    @blk.vector
                    def _(vec):
                        for c in range(NCH + 1):
                            if c <= NCH - 1:
                                vec.wait_ge(ld.h, ldwv(c))
                                if c >= 2:
                                    vec.wait_ge(ss.h, EVW[("sgn", c - 2)])
                                wsrc, wdst, r, c0, cw = chl[c]
                                _q_pre(vec, wrW[c % 2][:, 0:cw], cw // 16, bstats,
                                       88 * (c % 2), vWb[c % 2], sv)
                                assert sv.v == EVW[("v", c)]
                            if c >= 1:
                                b = c - 1
                                wsrc, wdst, r, c0, cw = chl[b]
                                vec.wait_ge(ss.h, EVW[("s3", b)])
                                vec.wait_ge(sq.h, EVW[("pool", b)])
                                if b >= 2:
                                    vec.wait_ge(stn.h, EVW[("t", b - 2)])
                                _q_post(vec, cw // 16, bstats, 88 * (b % 2), a2Wb[b % 2],
                                        sgnW, tabWb[b % 2], sc5,
                                        wqW[b % 2][:, 0:cw], sv,
                                        sgn_wait=(ss.h, EVW[("sgn", b)]))
                                assert sv.v == EVW[("post", b)]

                    @blk.gpsimd
                    def _(g):
                        for c in range(NCH):
                            g.wait_ge(ss.h, EVW[("s3", c)])
                            if c >= 2:
                                g.wait_ge(sv.h, EVW[("comb", c - 2)])
                            wsrc, wdst, r, c0, cw = chl[c]
                            _q_pool(g, s1W, s2W, s3W, tabWb[c % 2], cw, sq)
                            assert sq.v == EVW[("pool", c)]

                    @blk.tensor
                    def _(tensor):
                        for c in range(NCH):
                            assert stn.v == stnW + c
                            tensor.wait_ge(sv.h, EVW[("post", c)])
                            if c >= 1:
                                tensor.wait_ge(ss.h, EVW[("e", c - 1)])
                            wsrc, wdst, r, c0, cw = chl[c]
                            m = None
                            for k in range(cw // 128):
                                m = tensor.transpose(psT[:, 128 * k:128 * (k + 1)],
                                                     wqW[c % 2][:, 128 * k:128 * (k + 1)], ident[:])
                            stn.inc(m)

                    @blk.scalar
                    def _(scal):
                        for c in range(NCH + 1):
                            if c <= NCH - 1:
                                wsrc, wdst, r, c0, cw = chl[c]
                                scal.wait_ge(sv.h, EVW[("v", c)])
                                if c >= 2:
                                    scal.wait_ge(sv.h, EVW[("L3", c - 2)])
                                _q_scal(scal, vWb[c % 2][:, 0:cw], a2Wb[c % 2][:, 0:cw],
                                        s1W[:, 0:cw], s2W[:, 0:cw], s3W[:, 0:cw], qbias, ss,
                                        s_wait=(sq.h, EVW[("pool", c - 1)] - 2) if c >= 1 else None,
                                        s3_wait=(sq.h, EVW[("pool", c - 1)]) if c >= 1 else None)
                                assert ss.v == EVW[("s3", c)]
                            if c >= 1:
                                b = c - 1
                                wsrc, wdst, r, c0, cw = chl[b]
                                if b >= 1:
                                    scal.wait_ge(sv.h, EVW[("q1", b - 1)])
                                ss.inc(scal.activation(sgnW[:, 0:cw], vWb[b % 2][:, 0:cw], AF.Sign))
                                assert ss.v == EVW[("sgn", b)]
                                scal.wait_ge(stn.h, EVW[("t", b)])
                                wsrc, wdst, r, c0, cw = chl[b]
                                for k in range(cw // 128):
                                    kk = c0 // 128 + k
                                    ss.inc(scal.copy(wdst[:, kk, 128 * r:128 * (r + 1)],
                                                     psT[:, 128 * k:128 * (k + 1)]))
                                assert ss.v == EVW[("e", b)]

            g2a.close()
            pairbuf = g2sb("pairbuf", [128, 2, 2048], F32)
            import os as _os
            if _os.environ.get('KPRINT'):
                print('sbuf free bytes/partition during D:', nc.sbuf_bytes_remaining)

            nic_chunks = [(0, 512), (512, 512), (1024, 512), (1536, 512), (2048, 512), (2560, 256)]
            pair_cols = [(0, 1024), (1024, 1024), (2048, 768)]
            NTOT = NSEQ * NT if stage >= 7 else 0
            xqb = [sexp, attn_tm]
            pp = [sc0, sc1]
            rows = [(s * SEQ + 128 * tt) for s in range(NSEQ) for tt in range(NT)]
            svD, ssD, stnD, ldD, soutD, spD = sv.v, ss.v, stn.v, ld.v, sout.v, sp.v

            def cpf(gt):
                v = stnD
                for g in range(gt):
                    v += (3 if g <= NTOT - 1 else 0) + (4 if g >= 1 else 0)
                return v

            def csf(gt):
                v = ssD
                for g in range(gt):
                    v += (6 if g <= NTOT - 1 else 0) + (3 if g >= 1 else 0)
                return v

            def cvf(gt):
                v = svD
                for g in range(gt):
                    v += (63 if g <= NTOT - 1 else 0) + (2 if g >= 1 else 0)
                return v

            def pairv(gt, pq):
                return cpf(gt) + pq + 1

            def tgroup(gtm1, g):
                gt = gtm1 + 1
                return cpf(gt) + (3 if gt <= NTOT - 1 else 0) + g + 1

            def down_done(gtm1):
                return tgroup(gtm1, 2) + 1

            def spair(gt, pq, k):
                return csf(gt) + 2 * pq + k + 1

            def evacv(gtm1, g):
                gt = gtm1 + 1
                return csf(gt) + (6 if gt <= NTOT - 1 else 0) + g + 1

            def hmultv(gt, ci):
                pq, k = ci // 2, ci % 2
                return cvf(gt) + 21 * pq + k + 1

            def quantdonev(gt):
                return cvf(gt) + 63

            def addsv(gtm1):
                return cvf(gtm1 + 1) + 63 * (1 if gtm1 + 1 <= NTOT - 1 else 0) + 2

            def ldv(gt):
                return ldD + 32 * (gt + 1)

            gbank = [psQ, psB]
            if NTOT:
                with nc.Block() as blk:
                    @blk.sync
                    def _(sync):
                        for gt in range(NTOT):
                            if gt >= 1:
                                sync.wait_ge(ld.h, ldv(gt - 1))
                            if gt >= 2:
                                sync.wait_ge(sp.h, spD + gt - 1)
                                sync.wait_ge(stn.h, pairv(gt - 2, 2))
                            sync.dma_start(
                                out=xqb[gt % 2][:].rearrange("p (c t) -> p c t", t=128),
                                in_=xqT_dram[gt // NT, :, :, 128 * (gt % NT):128 * (gt % NT) + 128].rearrange("c p t -> p c t")).then_inc(ld.h, 16)
                            sync.dma_start(out=pp[gt % 2][:],
                                           in_=partial_dram[rows[gt]:rows[gt] + 128, :]).then_inc(ld.h, 16)
                            ld.v += 32

                    @blk.tensor
                    def _(tensor):
                        for gt in range(NTOT + 1):
                            assert stn.v == cpf(gt), (gt, stn.v, cpf(gt))
                            if gt <= NTOT - 1:
                                tensor.wait_ge(ld.h, ldv(gt))
                                for pq in range(3):
                                    m = None
                                    for sub in range(4):
                                        pgt, ppq = (gt, pq - 1) if pq >= 1 else (gt - 1, 2)
                                        if pq >= 1 or gt >= 1:
                                            if sub < 2:
                                                tensor.wait_ge(ss.h, spair(pgt, ppq, sub))
                                            else:
                                                tensor.wait_ge(sv.h, hmultv(pgt, 2 * ppq + sub - 2))
                                        k = sub % 2
                                        ci = 2 * pq + k
                                        i0, iw = nic_chunks[ci]
                                        if sub < 2:
                                            dst, w_src = gbank[k][:, 0:iw], wgqT
                                        else:
                                            dst = psU[:, 0:iw] if k == 0 else psB[:, 512:512 + iw]
                                            w_src = wuqT
                                        for c in range(NKC):
                                            m = tensor.matmul(dst, xqb[gt % 2][:, 128 * c:128 * (c + 1)],
                                                              w_src[:, c, i0:i0 + iw],
                                                              start=(c == 0), stop=(c == NKC - 1))
                                    stn.inc(m)
                                    assert stn.v == pairv(gt, pq)
                            if gt >= 1:
                                for g in range(3):
                                    tensor.wait_ge(sv.h, cvf(gt - 1) + 21 * (g + 1))
                                    if g == 0 and gt >= 2:
                                        tensor.wait_ge(ss.h, evacv(gt - 2, 2))
                                    if g > 0:
                                        tensor.wait_ge(ss.h, evacv(gt - 1, g - 1))
                                    ncg = min(8, NIC - 8 * g)
                                    m = None
                                    for c in range(ncg):
                                        m = tensor.transpose(
                                            psT[:, 128 * c:128 * (c + 1)],
                                            hq2[:, (gt - 1) % 2, 128 * (8 * g + c):128 * (8 * g + c + 1)],
                                            ident[:])
                                    stn.inc(m)
                                tensor.wait_ge(ss.h, evacv(gt - 1, 2))
                                if gt >= 2:
                                    tensor.wait_ge(sv.h, addsv(gt - 2))
                                m = None
                                for oc in range(2):
                                    for c in range(NIC):
                                        m = tensor.matmul(psA[:, 512 * oc:512 * (oc + 1)],
                                                          hqT[:, 128 * c:128 * (c + 1)],
                                                          wdqT[:, c, 512 * oc:512 * (oc + 1)],
                                                          start=(c == 0), stop=(c == NIC - 1))
                                stn.inc(m)
                                assert stn.v == down_done(gt - 1)

                    @blk.scalar
                    def _(scal):
                        for gt in range(NTOT + 1):
                            assert ss.v == csf(gt), (gt, ss.v, csf(gt))
                            if gt <= NTOT - 1:
                                for pq in range(3):
                                    scal.wait_ge(stn.h, pairv(gt, pq))
                                    if gt * 3 + pq >= 2:
                                        ppq = gt * 3 + pq - 2
                                        scal.wait_ge(sv.h, hmultv(ppq // 3, 2 * (ppq % 3) + 1))
                                    pb2 = pairbuf[:, (gt * 3 + pq) % 2, :]
                                    iwA = nic_chunks[2 * pq][1]
                                    iwB = nic_chunks[2 * pq + 1][1]
                                    ss.inc(scal.activation(pb2[:, 0:iwA], gbank[0][:, 0:iwA], AF.Silu))
                                    ss.inc(scal.activation(pb2[:, 512:512 + iwB], gbank[1][:, 0:iwB], AF.Silu))
                            if gt >= 1:
                                for g in range(3):
                                    scal.wait_ge(stn.h, tgroup(gt - 1, g))
                                    ncg = min(8, NIC - 8 * g)
                                    ss.inc(scal.copy(hqT[:, 128 * 8 * g:128 * (8 * g + ncg)],
                                                     psT[:, 0:128 * ncg]))

                    @blk.vector
                    def _(vec):
                        for gt in range(NTOT + 1):
                            assert sv.v == cvf(gt), (gt, sv.v, cvf(gt))
                            if gt <= NTOT - 1:
                                if gt >= 2:
                                    vec.wait_ge(stn.h, tgroup(gt - 2, 2))
                                for pq in range(3):
                                    pb2 = pairbuf[:, (gt * 3 + pq) % 2, :]
                                    iwA = nic_chunks[2 * pq][1]
                                    iwB = nic_chunks[2 * pq + 1][1]
                                    vec.wait_ge(stn.h, pairv(gt, pq))
                                    vec.wait_ge(ss.h, spair(gt, pq, 0))
                                    sv.inc(vec.tensor_mul(sc2[:, 0:iwA], pb2[:, 0:iwA], psU[:, 0:iwA]))
                                    vec.wait_ge(ss.h, spair(gt, pq, 1))
                                    sv.inc(vec.tensor_mul(sc2[:, iwA:iwA + iwB], pb2[:, 512:512 + iwB],
                                                          psB[:, 512:512 + iwB]))
                                    pw = iwA + iwB
                                    _emit_quant(vec, sc2[:, 0:pw], pw // 16, bstats,
                                                sc3, sc4, sc5, sc6,
                                                hq2[:, gt % 2, pair_cols[pq][0]:pair_cols[pq][0] + pw], sv)
                            if gt >= 1:
                                vec.wait_ge(stn.h, down_done(gt - 1))
                                vec.wait_ge(ld.h, ldv(gt - 1))
                                b = (gt - 1) % 2
                                sv.inc(vec.tensor_add(pp[b][:, 0:512], psA[:, 0:512], pp[b][:, 0:512]))
                                sv.inc(vec.tensor_add(pp[b][:, 512:1024], psA[:, 512:1024], pp[b][:, 512:1024]))
                                assert sv.v == addsv(gt - 1)

                    @blk.gpsimd
                    def _(gpsimd):
                        for gt in range(1, NTOT + 1):
                            gpsimd.wait_ge(sv.h, addsv(gt - 1))
                            gpsimd.dma_start(out=out[rows[gt - 1]:rows[gt - 1] + 128, :],
                                             in_=pp[(gt - 1) % 2][:]).then_inc(sout.h, 16)
                            sout.v += 16
                            gpsimd.wait_ge(sout.h, soutD + 16 * gt)
                            gpsimd.sem_inc(sp.h, 1)
                            sp.v += 1

            with nc.Block() as blk:
                @blk.gpsimd
                def _(gpsimd):
                    gpsimd.wait_ge(sout.h, sout.v)

    return nc


def _ssel():
    s = np.zeros((128, 128), dtype=ml_dtypes.bfloat16)
    s[0, 0:64] = 1.0
    s[64, 64:128] = 1.0
    return s


def kernel(L0_attn_x, L0_attn_Wqkv, L0_attn_bqkv, L0_attn_Wproj, L0_attn_bproj,
           L0_attn_mapping, L0_ffn_w_gate, L0_ffn_w_up, L0_ffn_w_down):
    x = np.ascontiguousarray(np.asarray(L0_attn_x, dtype=np.float32).reshape(16 * SEQ, D))
    consts = dict(
        Wqkv=np.ascontiguousarray(np.asarray(L0_attn_Wqkv, np.float32)),
        bqkv=np.ascontiguousarray(np.asarray(L0_attn_bqkv, np.float32).reshape(1, I3)),
        Wproj=np.ascontiguousarray(np.asarray(L0_attn_Wproj, np.float32)),
        bproj=np.ascontiguousarray(np.asarray(L0_attn_bproj, np.float32).reshape(1, D)),
        mapping=np.ascontiguousarray(np.asarray(L0_attn_mapping, np.int32).reshape(4096, 1)),
        wg=np.ascontiguousarray(np.asarray(L0_ffn_w_gate, np.float32)),
        wu=np.ascontiguousarray(np.asarray(L0_ffn_w_up, np.float32)),
        wd=np.ascontiguousarray(np.asarray(L0_ffn_w_down, np.float32)),
        ident_in=np.eye(128, dtype=ml_dtypes.bfloat16),
        ones_in=np.ones((1, 128), dtype=ml_dtypes.bfloat16),
        etop_in=np.concatenate([np.ones((1, 64)), np.zeros((1, 64))], axis=1).astype(ml_dtypes.bfloat16),
        ebot_in=np.concatenate([np.zeros((1, 64)), np.ones((1, 64))], axis=1).astype(ml_dtypes.bfloat16),
        ssel_in=_ssel(),
    )
    nc = build_nc()
    in_maps = []
    for c in range(8):
        m = dict(consts)
        m["xin"] = np.ascontiguousarray(x[c * NSEQ * SEQ:(c + 1) * NSEQ * SEQ, :])
        in_maps.append(m)
    res = run_bass_kernel_spmd(nc, in_maps, core_ids=list(range(8)))
    outs = [np.asarray(res.results[c]["out"]) for c in range(8)]
    return np.concatenate(outs, axis=0).reshape(16, SEQ, D)
